# version 47
# baseline (speedup 1.0000x reference)
from contextlib import ExitStack

import numpy as np

import concourse.bass as bass
import concourse.mybir as mybir
from concourse.bass_utils import run_bass_kernel_spmd

F32 = mybir.dt.float32
BF16 = mybir.dt.bfloat16
AF = mybir.ActivationFunctionType
ALU = mybir.AluOpType
AX = mybir.AxisListType
NEG = -1.0e30

D = 1024
KC = 8
ROT = 32
EPS = 1e-5


class CFG:
    def __init__(self, own=2048, qown=512):
        self.own = own
        self.halo = 128
        self.T = own + 2 * self.halo
        self.qown = qown
        self.nq = own // qown
        self.R = qown + 256
        self.qpq = qown // 128
        self.ttiles = self.T // 128
        self.last_qt = own // 128 - 1

    def tchunks(self):
        out, t = [], 0
        while t < self.T:
            w = min(512, self.T - t)
            out.append((t, w))
            t += w
        return out

    def rsubs(self):
        out, t = [], 0
        while t < self.R:
            w = min(512, self.R - t)
            out.append((t, w))
            t += w
        return out


ENGINES = ("pe", "dve", "act", "pool", "sp")


class B:

    def __init__(self, nc, stack):
        self.nc = nc
        self.ops = {e: [] for e in ENGINES}
        self.sems = {}
        self.count = {}
        self.hw = {}
        self.stack = stack

    def sem(self, name):
        self.sems[name] = self.stack.enter_context(self.nc.semaphore(name))
        self.count[name] = 0

    def emit(self, eng, fn, inc=None, n=1):
        if inc is None:
            self.ops[eng].append(fn)
            return 0
        self.count[inc] += n
        s = self.sems[inc]
        self.ops[eng].append(lambda E, fn=fn, s=s, n=n: fn(E).then_inc(s, n))
        return self.count[inc]

    def wait(self, eng, sem, val=None):
        if val is None:
            val = self.count[sem]
        if val <= 0:
            return
        key = (eng, sem)
        if self.hw.get(key, 0) >= val:
            return
        self.hw[key] = val
        s = self.sems[sem]
        self.ops[eng].append(lambda E, s=s, v=val: E.wait_ge(s, v))

    def mm(self, ps, lt, rs, st, sp, tile_position=None):
        return self.emit(
            "pe",
            lambda E, a=ps, b=lt, c=rs, st=st, sp=sp, tp=tile_position:
                E.matmul(a, b, c, start=st, stop=sp, tile_position=tp),
            inc="pe")

    def tr(self, ps, in_, ident):
        return self.emit(
            "pe", lambda E, a=ps, b=in_, c=ident: E.transpose(a, b, c),
            inc="pe")

    def tt(self, eng, out, a0, a1, op):
        inc = "dve" if eng == "dve" else "pool"
        return self.emit(
            eng, lambda E, o=out, x=a0, y=a1, op=op: E.tensor_tensor(o, x, y, op),
            inc=inc)

    def ts(self, eng, out, a0, s1, op):
        inc = "dve" if eng == "dve" else "pool"
        return self.emit(
            eng,
            lambda E, o=out, x=a0, s=s1, op=op: E.tensor_scalar(o, x, s, None, op),
            inc=inc)

    def stt(self, eng, out, a0, s, a1, op0, op1):
        inc = {"dve": "dve", "act": "act"}.get(eng, "pool")
        return self.emit(
            eng,
            lambda E, o=out, x=a0, s=s, y=a1, p=op0, q=op1:
                E.scalar_tensor_tensor(o, x, s, y, p, q),
            inc=inc)

    def cp(self, eng, out, in_):
        if eng == "act":
            return self.emit(
                "act", lambda E, o=out, i=in_: E.copy(o, i), inc="act")
        inc = "dve" if eng == "dve" else "pool"
        return self.emit(
            eng, lambda E, o=out, i=in_: E.tensor_copy(o, i), inc=inc)

    def red(self, out, in_, op=ALU.max, negate=False):
        return self.emit(
            "dve",
            lambda E, o=out, i=in_, op=op, ng=negate:
                E.tensor_reduce(o, i, AX.X, op, negate=ng),
            inc="dve")

    def actv(self, out, in_, func, bias=0.0, scale=1.0, accum=None):
        return self.emit(
            "act",
            lambda E, o=out, i=in_, f=func, b=bias, s=scale, a=accum:
                E.activation(o, i, f, bias=b, scale=s, accum_out=a),
            inc="act")

    def recip(self, out, in_):
        return self.emit(
            "dve", lambda E, o=out, i=in_: E.reciprocal(o, i), inc="dve")

    def dma(self, eng, sem, out, in_):
        return self.emit(
            eng, lambda E, o=out, i=in_: E.dma_start(o, i), inc=sem, n=16)

    def memset(self, eng, ap, val):
        inc = "dve" if eng == "dve" else "pool"
        return self.emit(
            eng, lambda E, a=ap, v=val: E.memset(a, v), inc=inc)


def build_program(cfg: CFG, phases=("ffn0", "attn", "ffn1")):
    nc = bass.Bass("TRN2", target_bir_lowering=False)
    T, R = cfg.T, cfg.R

    xT_d = nc.dram_tensor("xT", [D, T], F32, kind="ExternalInput").ap()
    wqkv_d = nc.dram_tensor("wqkv", [D, 3 * D], F32, kind="ExternalInput").ap()
    wout_d = nc.dram_tensor("wout", [D, D], F32, kind="ExternalInput").ap()
    bout_d = nc.dram_tensor("boutT", [128, 8], F32, kind="ExternalInput").ap()
    gwT_d = nc.dram_tensor("gwT", [D, 8], F32, kind="ExternalInput").ap()
    wexp_d = nc.dram_tensor("wexpT", [8, D, D], F32, kind="ExternalInput").ap()
    ident_d = nc.dram_tensor("ident", [128, 128], F32, kind="ExternalInput").ap()
    pswap_d = nc.dram_tensor("pswap", [128, 128], F32, kind="ExternalInput").ap()
    crep_d = nc.dram_tensor("crep", [128, T], F32, kind="ExternalInput").ap()
    srep_d = nc.dram_tensor("srep", [128, T], F32, kind="ExternalInput").ap()
    biasq_d = nc.dram_tensor("biasq", [128, 3 * 384], F32, kind="ExternalInput").ap()
    out_d = nc.dram_tensor("outT", [D, cfg.own], F32, kind="ExternalOutput").ap()
    webf_d = nc.dram_tensor("wexp_bf", [8, 128, KC * D], BF16,
                            kind="Internal").ap()
    wqbf_d = nc.dram_tensor("wqkv_bf", [4, 128, KC * D], BF16,
                            kind="Internal").ap()
    oht_d = nc.dram_tensor("onehot_sc", [8, T], BF16, kind="Internal").ap()

    stack = ExitStack()
    b = B(nc, stack)
    for s in ("pe", "dve", "act", "pool", "dsp", "dpl", "w0", "w1", "w2",
              "w3", "tee0", "tee1", "tee2", "tee3", "mk0", "mk1", "dso",
              "dx0", "dx1", "dx2", "dx3", "dx4", "dx5", "dx6", "dx7"):
        b.sem(s)

    def sbuf(name, shape, dt):
        return stack.enter_context(nc.sbuf_tensor(name, shape, dt))

    x_sb = sbuf("x_sb", [128, KC * T], F32)
    bout_sb = sbuf("bout_sb", [128, 8], F32)
    gwT_sb = sbuf("gwT_sb", [128, KC * 8], F32)
    idb = sbuf("idb", [128, 128], BF16)
    pswap_sb = sbuf("pswap_sb", [128, 128], BF16)
    biasq_sb = sbuf("biasq_sb", [128, 3 * 384], BF16)
    ones_bf = sbuf("ones_bf", [128, 1], BF16)
    onescol = sbuf("onescol", [128, 1], F32)
    ones1 = sbuf("ones1", [1, 128], F32)
    xb_g = sbuf("xb_g", [128, 2048], BF16)
    xq_g = sbuf("xq_g", [128, 2048], BF16)
    rowtmp = sbuf("rowtmp", [1, T], F32)

    for k in range(KC):
        b.dma("sp", f"dx{k}", x_sb[:, k * T:(k + 1) * T],
              xT_d[k * 128:(k + 1) * 128, :])
    b.dma("sp", "dsp", bout_sb[:], bout_d[:])
    for k in range(KC):
        b.dma("sp", "dsp", gwT_sb[:, k * 8:(k + 1) * 8], gwT_d[k * 128:(k + 1) * 128, :])
    init_sp = b.count["dsp"]
    b.dma("pool", "dpl", idb[:], ident_d[:])
    b.dma("pool", "dpl", pswap_sb[:], pswap_d[:])
    b.dma("pool", "dpl", biasq_sb[:], biasq_d[:])
    init_pl = b.count["dpl"]
    b.memset("dve", ones_bf[:], 1.0)
    b.memset("dve", onescol[:], 1.0)
    b.memset("dve", ones1[:], 1.0)
    init_dve = b.count["dve"]

    def emit_stats(tag, a, L, meanr, rstdr, rmr):
        b.wait("dve", "dve", b.count["dve"])
        b.wait("act", "dve", b.count["dve"])
        with ExitStack() as pscope:
            p1 = pscope.enter_context(nc.psum_tensor(f"p1{tag}", [1, 512], F32))
            p2 = pscope.enter_context(nc.psum_tensor(f"p2{tag}", [1, 512], F32))
            t = 0
            pe_slot = {i: 0 for i in range(4)}
            while t < L:
                w = min(512, L - t)
                for k in range(KC):
                    sl = k % 4
                    if pe_slot[sl]:
                        b.wait("dve", "pe", pe_slot[sl])
                        b.wait("act", "pe", pe_slot[sl])
                    b.wait("dve", f"dx{k}", 16)
                    b.wait("act", f"dx{k}", 16)
                    cast_c = b.cp("dve", xb_g[:, sl * 512: sl * 512 + w],
                                  x_sb[:, k * T + a + t: k * T + a + t + w])
                    sq_c = b.actv(xq_g[:, sl * 512: sl * 512 + w],
                                  x_sb[:, k * T + a + t: k * T + a + t + w],
                                  AF.Square)
                    b.wait("pe", "dve", cast_c)
                    b.wait("pe", "act", sq_c)
                    b.mm(p1[:, 0:w], ones_bf[:], xb_g[:, sl * 512: sl * 512 + w],
                         k == 0, k == KC - 1)
                    pe_slot[sl] = b.mm(
                        p2[:, 0:w], ones_bf[:], xq_g[:, sl * 512: sl * 512 + w],
                        k == 0, k == KC - 1)
                b.wait("dve", "pe", b.count["pe"])
                b.cp("dve", meanr[0:1, t:t + w], p1[:, 0:w])
                ev = b.cp("dve", rstdr[0:1, t:t + w], p2[:, 0:w])
                b.wait("pe", "dve", ev)
                t += w
        b.wait("act", "dve", b.count["dve"])
        c = b.actv(meanr, meanr, AF.Copy, scale=1.0 / D)
        c = b.actv(rstdr, rstdr, AF.Copy, scale=1.0 / D)
        tmp = rowtmp[0:1, 0:L]
        b.wait("act", "act", c)
        c = b.actv(tmp, meanr, AF.Square)
        b.wait("dve", "act", c)
        c = b.tt("dve", rstdr, rstdr, tmp, ALU.subtract)
        b.wait("dve", "dve", c)
        c = b.ts("dve", rstdr, rstdr, EPS, ALU.add)
        b.wait("act", "dve", c)
        c = b.actv(tmp, rstdr, AF.Ln)
        b.wait("act", "act", c)
        c = b.actv(rstdr, tmp, AF.Exp, scale=-0.5)
        if rmr is not None:
            b.wait("dve", "act", c)
            b.tt("dve", rmr, meanr, rstdr, ALU.mult)

    def emit_bcast(row, L, dst, pbank):
        t = 0
        while t < L:
            w = min(512, L - t)
            b.wait("pe", "dve", b.count["dve"])
            b.wait("pe", "act", b.count["act"])
            mmc = b.mm(pbank[:, 0:w], ones1[:], row[0:1, t:t + w], True, True)
            b.wait("dve", "pe", mmc)
            ev = b.cp("dve", dst[:, t:t + w], pbank[:, 0:w])
            b.wait("pe", "dve", ev)
            t += w

    def emit_cols(row, ntiles, dst, pbank):
        b.wait("pe", "dve", b.count["dve"])
        b.wait("pe", "act", b.count["act"])
        for tt in range(ntiles):
            b.mm(pbank[:, tt: tt + 1],
                 row[0:1, tt * 128:(tt + 1) * 128], ones1[0:1, 0:1],
                 True, True)
        b.wait("dve", "pe", b.count["pe"])
        ev = b.cp("dve", dst[:, 0:ntiles], pbank[:, 0:ntiles])
        b.wait("pe", "dve", ev)

    def emit_ffn(ffn_idx):
        tag = f"f{ffn_idx}"
        with ExitStack() as fs:
            sb = lambda n, sh, dt: fs.enter_context(
                nc.sbuf_tensor(n + tag, sh, dt))
            meanr = sb("meanr", [1, T], F32)
            rstdr = sb("rstdr", [1, T], F32)
            rmr = rowtmp[0:1, 0:T]
            emit_stats(tag, 0, T, meanr[:], rstdr[:], rmr)

            hrb_sb = sb("hrb", [128, T], F32)
            mb_sb = sb("mb", [128, T], F32)
            rcol = sb("rcol", [128, cfg.ttiles], F32)
            rmcol = sb("rmcol", [128, cfg.ttiles], F32)

            onehotT = sb("onehotT", [8, T], BF16)
            mkbuf = sb("mkbuf", [128, 2 * 8 * 512], BF16)
            sel8 = sb("sel8", [8, 1024], BF16)
            b.wait("dve", "dpl", init_pl)
            for e in range(8):
                b.cp("dve", sel8[0:8, e * 128:(e + 1) * 128],
                     idb[0:8, e:e + 1].broadcast_to((8, 128)))
            g1T = sb("g1T", [128, 8], F32)
            g1r = sb("g1r", [1, 8], F32)
            nt = cfg.ttiles
            ohbuf = sb("ohbuf", [128, 8 * nt], BF16)
            u144 = sb("u144", [128, 8 * nt], F32)
            v144 = sb("v144", [128, 8 * nt], F32)
            c144 = sb("c144", [128, 8 * nt], F32)
            m18 = sb("m18", [128, nt], F32)
            with ExitStack() as pscope:
                pg1 = pscope.enter_context(
                    nc.psum_tensor("pg1" + tag, [1, 8], F32))
                pgb = pscope.enter_context(
                    nc.psum_tensor("pgb" + tag, [128, 8], F32))
                pg = pscope.enter_context(
                    nc.psum_tensor("pg" + tag, [128, 8 * nt], F32))
                pot = [pscope.enter_context(
                    nc.psum_tensor(f"pot{pp}" + tag, [8, 128], BF16))
                    for pp in range(2)]
                pmk = pscope.enter_context(
                    nc.psum_tensor("pmk" + tag, [128, 1024], F32))
                b.wait("pe", "dsp", init_sp)
                b.wait("pe", "dve", init_dve)
                for k in range(KC):
                    b.mm(pg1[:], onescol[:], gwT_sb[:, k * 8:(k + 1) * 8],
                         k == 0, k == KC - 1)
                b.wait("dve", "pe", b.count["pe"])
                ev = b.cp("dve", g1r[:], pg1[:])
                b.wait("pe", "dve", ev)
                mmc = b.mm(pgb[:], ones1[:], g1r[:], True, True)
                b.wait("dve", "pe", mmc)
                b.cp("dve", g1T[:], pgb[:])
                for tt in range(nt):
                    for k in range(KC):
                        b.wait("pe", f"dx{k}", 16)
                        b.mm(pg[:, tt * 8: tt * 8 + 8],
                             x_sb[:, k * T + tt * 128: k * T + (tt + 1) * 128],
                             gwT_sb[:, k * 8:(k + 1) * 8],
                             k == 0, k == KC - 1)
                gate_pe = b.count["pe"]
                with ExitStack() as bscope:
                    pb = bscope.enter_context(
                        nc.psum_tensor("pb" + tag, [128, 512], F32))
                    emit_cols(rstdr[:], cfg.ttiles, rcol, pb)
                    emit_cols(rmr, cfg.ttiles, rmcol, pb)
                    emit_bcast(meanr[:], T, mb_sb, pb)
                    b.wait("dve", "act", b.count["act"])
                    c = b.ts("dve", rstdr[:], rstdr[:], 0.5, ALU.mult)
                    b.wait("dve", "dve", c)
                    emit_bcast(rstdr[:], T, hrb_sb, pb)
                b.wait("dve", "pe", gate_pe)
                b.wait("dve", "dve", b.count["dve"])
                pg3 = pg[:].rearrange("p (t e) -> p t e", e=8)
                u3 = u144[:].rearrange("p (t e) -> p t e", e=8)
                v3 = v144[:].rearrange("p (t e) -> p t e", e=8)
                c3 = c144[:].rearrange("p (t e) -> p t e", e=8)
                oh3 = ohbuf[:].rearrange("p (t e) -> p t e", e=8)
                rcol_b = rcol[:, 0:nt].unsqueeze(2).broadcast_to((128, nt, 8))
                rmcol_b = rmcol[:, 0:nt].unsqueeze(2).broadcast_to((128, nt, 8))
                g1_b = g1T[:].unsqueeze(1).broadcast_to((128, nt, 8))
                m_b = m18[:, 0:nt].unsqueeze(2).broadcast_to((128, nt, 8))
                c = b.tt("dve", u3, pg3, rcol_b, ALU.mult)
                b.wait("dve", "dve", c)
                c = b.tt("dve", v3, g1_b, rmcol_b, ALU.mult)
                b.wait("dve", "dve", c)
                c = b.tt("dve", u3, u3, v3, ALU.subtract)
                b.wait("dve", "dve", c)
                c = b.red(m18[:, 0:nt].unsqueeze(2), u3)
                b.wait("dve", "dve", c)
                c = b.tt("dve", v3, u3, m_b, ALU.is_equal)
                b.wait("dve", "dve", c)
                c = b.cp("dve", c3[:, :, 0:1], v3[:, :, 0:1])
                for e in range(1, 8):
                    b.wait("dve", "dve", c)
                    c = b.tt("dve", c3[:, :, e:e + 1], c3[:, :, e - 1:e],
                             v3[:, :, e:e + 1], ALU.add)
                b.wait("dve", "dve", c)
                c = b.ts("dve", u3, c3, 1.0, ALU.is_equal)
                b.wait("dve", "dve", c)
                c = b.tt("dve", oh3, v3, u3, ALU.mult)
                b.wait("dve", "dve", c)
                b.wait("pe", "dve", b.count["dve"])
                b.wait("pe", "dpl", init_pl)
                last_ev = {0: 0, 1: 0}
                oh4_dve = 0
                for tt in range(nt):
                    sl = tt % 2
                    if last_ev[sl]:
                        b.wait("pe", "dve", last_ev[sl])
                    trc = b.tr(pot[sl][:],
                               ohbuf[:, tt * 8: tt * 8 + 8], idb[:])
                    b.wait("dve", "pe", trc)
                    last_ev[sl] = b.cp(
                        "dve", onehotT[:, tt * 128:(tt + 1) * 128],
                        pot[sl][:])
                    if tt == 3:
                        oh4_dve = b.count["dve"]
                w0c = cfg.tchunks()[0][1]
                b.wait("pe", "dve", oh4_dve)
                mk0d = mk0a = 0
                mkev = {0: 0, 1: 0}
                for e in range(8):
                    sl = e % 2
                    if mkev[sl]:
                        b.wait("pe", "dve" if sl == 0 else "act", mkev[sl])
                    mmc = b.mm(pmk[:, sl * 512: sl * 512 + w0c],
                               sel8[:, e * 128:(e + 1) * 128],
                               onehotT[0:8, 0:w0c], True, True)
                    eng = "dve" if sl == 0 else "act"
                    b.wait(eng, "pe", mmc)
                    c = b.cp(eng, mkbuf[:, e * 512: e * 512 + w0c],
                             pmk[:, sl * 512: sl * 512 + w0c])
                    mkev[sl] = c
                    if sl == 0:
                        mk0d = c
                    else:
                        mk0a = c

            b.wait("sp", "dve", b.count["dve"])
            oh_w = b.dma("sp", "dsp", oht_d[:, :], onehotT[:, :])

            xc = sb("xc", [128, 2 * KC * 512], BF16)
            webuf = sb("webuf", [128, 4 * 2048], BF16)
            lmbuf = sb("lmbuf", [128, 4 * 1024], BF16)
            ev8 = sb("ev8", [128, KC * 512], BF16)
            uscr = sb("uscr", [128, 2 * 512], F32)
            pd = fs.enter_context(nc.psum_tensor("pd" + tag, [128, 4096], F32))
            wsem = ["w0", "w1", "w2", "w3"]
            tsem = ["tee0", "tee1", "tee2", "tee3"]
            wload = [b.count[s] // 16 for s in wsem]
            wfree = [0, 0, 0, 0]
            wtee = [0, 0, 0, 0]
            lmfree = [0, 0, 0, 0]
            sidx = 0
            piece_tee = [0] * 32
            tcs = cfg.tchunks()

            x3 = x_sb[:].rearrange("p (k t) -> p k t", t=T)
            pd3 = pd[:].rearrange("p (d f) -> p d f", f=512)
            ev3 = ev8[:].rearrange("p (d f) -> p d f", f=512)
            mkload = [0, 0]

            def xc3(par):
                return xc[:, par * KC * 512:(par + 1) * KC * 512].rearrange(
                    "p (k f) -> p k f", f=512)

            def gen_xc(ti, t0, w, after=0):
                b.wait("dve", "dve", max(after, b.count["dve"]))
                return b.tt(
                    "dve", xc3(ti % 2)[:, :, 0:w], x3[:, :, t0:t0 + w],
                    mb_sb[:, t0:t0 + w].unsqueeze(1).broadcast_to((128, KC, w)),
                    ALU.subtract)

            def load_masks(ti, t0, w):
                mp = ti % 2
                msem = "mk0" if mp == 0 else "mk1"
                b.wait("sp", "dsp", oh_w)
                for e in range(8):
                    mkload[mp] = b.dma(
                        "sp", msem,
                        mkbuf[:, mp * 4096 + e * 512: mp * 4096 + e * 512 + w],
                        oht_d[e:e + 1, t0:t0 + w].broadcast_to((128, w)))
                return mkload[mp]

            def emit_phase2_d(d, t0, w, p1cnt, p1acnt):
                eng = "dve" if d < 4 else "pool"
                sl_ = 0 if d < 4 else 1
                us = uscr[:, sl_ * 512: sl_ * 512 + w]
                b.wait(eng, "dve", p1cnt)
                b.wait(eng, "act", p1acnt)
                b.wait(eng, eng, b.count[eng])
                cc = b.tt(eng, us, x_sb[:, d * T + t0: d * T + t0 + w],
                          mb_sb[:, t0:t0 + w], ALU.subtract)
                b.wait(eng, eng, cc)
                cc = b.tt(eng, us, us, ev8[:, d * 512: d * 512 + w], ALU.add)
                b.wait(eng, eng, cc)
                cc = b.tt(eng, us, us, hrb_sb[:, t0:t0 + w], ALU.mult)
                b.wait(eng, eng, cc)
                cc = b.tt(eng, x_sb[:, d * T + t0: d * T + t0 + w],
                          x_sb[:, d * T + t0: d * T + t0 + w],
                          us, ALU.add)
                return cc, eng

            xc_dve = gen_xc(0, tcs[0][0], tcs[0][1])
            mk_w = None
            xc_next = xc_dve
            mk_next = mk_w
            lm_last = [0, 0]
            pending = []
            drain_w = {}
            ph2 = {"dve": 0, "pool": 0}
            for ti, (t0, w) in enumerate(tcs):
                xc_dve, mk_w = xc_next, mk_next
                mp = ti % 2
                msem = "mk0" if mp == 0 else "mk1"
                b.wait("pe", "dve", max(xc_dve, ph2["dve"]))
                b.wait("pe", "pool", ph2["pool"])
                for e in range(8):
                    if pending:
                        cc, eng = pending.pop(0)()
                        ph2[eng] = cc
                    if e == 5 and ti + 1 < len(tcs):
                        b.wait("sp", "dve", lm_last[(ti + 1) % 2])
                        mk_next = load_masks(ti + 1, tcs[ti + 1][0],
                                             tcs[ti + 1][1])
                    if e == 6 and ti + 1 < len(tcs):
                        xc_next = gen_xc(ti + 1, tcs[ti + 1][0], tcs[ti + 1][1])
                    for h in range(4):
                        buf = sidx % 4
                        if wfree[buf]:
                            b.wait("pool", "pe", wfree[buf])
                        if wtee[buf]:
                            b.wait("pool", tsem[buf], wtee[buf])
                        if ffn_idx == 0 and ti == 0:
                            dst = webuf[:, buf * 2048:
                                        (buf + 1) * 2048].rearrange(
                                "p (k f) -> p k f", f=1024)
                            src = wexp_d[e, h * 256:(h + 1) * 256, :].rearrange(
                                "(k p) f -> p k f", p=128)
                        else:
                            piece = e * 4 + h
                            if ffn_idx == 0:
                                b.wait("pool", tsem[piece % 4],
                                       piece_tee[piece])
                            else:
                                for ts_ in tsem:
                                    b.wait("pool", ts_, b.count[ts_])
                            dst = webuf[:, buf * 2048:(buf + 1) * 2048]
                            src = webf_d[e, :, h * 2048:(h + 1) * 2048]
                        wload[buf] += 1
                        b.dma("pool", wsem[buf], dst, src)
                        if ffn_idx == 0 and ti == 0:
                            b.wait("sp", wsem[buf], 16 * wload[buf])
                            wtee[buf] = b.dma(
                                "sp", tsem[buf],
                                webf_d[e, :, h * 2048:(h + 1) * 2048],
                                webuf[:, buf * 2048:(buf + 1) * 2048])
                            piece_tee[e * 4 + h] = wtee[buf]
                        b.wait("pe", wsem[buf], 16 * wload[buf])
                        sl2 = (e * 4 + h) % 4
                        if lmfree[sl2]:
                            b.wait("dve", "pe", lmfree[sl2])
                        b.wait("dve", "dve", xc_dve)
                        if mk_w is None:
                            b.wait("dve", "dve", mk0d)
                            b.wait("dve", "act", mk0a)
                        else:
                            b.wait("dve", msem, mk_w)
                        lm3 = lmbuf[:, sl2 * 1024:(sl2 + 1) * 1024].rearrange(
                            "p (k f) -> p k f", f=512)
                        lmr = b.tt(
                            "dve", lm3[:, :, 0:w],
                            xc3(mp)[:, 2 * h:2 * h + 2, 0:w],
                            mkbuf[:, mp * 4096 + e * 512:
                                  mp * 4096 + e * 512 + w]
                            .unsqueeze(1).broadcast_to((128, 2, w)),
                            ALU.mult)
                        b.wait("pe", "dve", lmr)
                        pec = 0
                        for kk in range(2):
                            k = h * 2 + kk
                            first = (e == 0 and k == 0)
                            last = (e == 7 and k == 7)
                            for d in range(KC):
                                if first and d in drain_w:
                                    eng2, c2 = drain_w.pop(d)
                                    b.wait("pe", eng2, c2)
                                pec = b.mm(
                                    pd[:, d * 512: d * 512 + w],
                                    webuf[:, buf * 2048 + kk * 1024 + d * 128:
                                          buf * 2048 + kk * 1024 + (d + 1) * 128],
                                    lmbuf[:, sl2 * 1024 + kk * 512:
                                          sl2 * 1024 + kk * 512 + w],
                                    first, last)
                        lmfree[sl2] = pec
                        wfree[buf] = pec
                        sidx += 1
                lm_last[mp] = b.count["dve"]
                b.wait("dve", "pe", b.count["pe"])
                b.wait("act", "pe", b.count["pe"])
                p1d1 = b.cp("dve", ev3[:, 0:2, 0:w], pd3[:, 0:2, 0:w])
                p1 = b.cp("dve", ev3[:, 2:4, 0:w], pd3[:, 2:4, 0:w])
                p1a1 = b.cp("act", ev3[:, 4:6, 0:w], pd3[:, 4:6, 0:w])
                p1a = b.cp("act", ev3[:, 6:8, 0:w], pd3[:, 6:8, 0:w])
                drain_w.clear()
                drain_w.update({0: ("dve", p1d1), 2: ("dve", p1),
                                4: ("act", p1a1), 6: ("act", p1a)})
                pending = [
                    (lambda d=d, t0=t0, w=w, p1=p1, p1a=p1a:
                     emit_phase2_d(d, t0, w, p1, p1a))
                    for d in range(KC)]
                if ti + 1 == len(tcs):
                    for fn in pending:
                        cc, eng = fn()
                        ph2[eng] = cc
                    pending = []


    def emit_attn():
        with ExitStack() as ats:
            sb = lambda n, sh, dt: ats.enter_context(
                nc.sbuf_tensor(n + "_at", sh, dt))
            crep_sb = sb("crep", [128, R], BF16)
            srep_sb = sb("srep", [128, R], BF16)

            wbufA = sb("wbufA", [128, 8192], BF16)
            wbufB = sb("wbufB", [128, 8192], BF16)
            wb = {"A": [wbufA, "w0", 0, 0, 0, ""],
                  "B": [wbufB, "w1", 0, 0, 0, ""]}
            wb["A"][2] = b.count["w0"] // 16
            wb["B"][2] = b.count["w1"] // 16
            att_teed = [0, 0, 0, 0]

            def wload(which, m, src2d):
                ent = wb[which]
                if ent[3]:
                    b.wait("pool", "pe", ent[3])
                if ent[4]:
                    b.wait("pool", ent[5], ent[4])
                ent[2] += 1
                if att_teed[m]:
                    b.wait("pool", f"tee{m}", att_teed[m])
                    b.dma("pool", ent[1], ent[0][:], wqbf_d[m, :, :])
                else:
                    dst = ent[0][:].rearrange("p (k f) -> p k f", f=1024)
                    src = src2d.rearrange("(k p) f -> p k f", p=128)
                    b.dma("pool", ent[1], dst, src)
                    b.wait("sp", ent[1], 16 * ent[2])
                    att_teed[m] = b.dma("sp", f"tee{m}", wqbf_d[m, :, :],
                                        ent[0][:])
                    ent[4] = att_teed[m]
                    ent[5] = f"tee{m}"
                return 16 * ent[2]

            def wwait(which, v):
                b.wait("pe", wb[which][1], v)

            def wdone(which):
                wb[which][3] = b.count["pe"]

            nvt = R // 128
            k_sb = sb("k_sb", [128, KC * R], BF16)
            v_sb = sb("v_sb", [128, nvt * 1024], BF16)
            q_sb = sb("q_sb", [128, KC * cfg.qown], BF16)
            xc2 = sb("xc2", [128, KC * R], BF16)
            outq = sb("outq", [128, KC * 128], F32)
            qtw = cfg.qpq * 128
            attnq4 = sb("attnq4", [128, KC * qtw], BF16)
            qraw = sb("qraw", [128, 2048], BF16)
            rt1 = sb("rt1", [128, 1024], BF16)
            rt2 = qraw
            meanr = sb("ameanr", [1, R], F32)
            rstdr = sb("arstdr", [1, R], F32)
            rcolq = sb("rcolq", [128, nvt], F32)
            p32 = sb("p32", [128, 2 * 768], F32)
            pn_bf = sb("pn_bf", [128, 2 * 768], BF16)
            pt_sb = sb("pt_sb", [128, 2 * 768], BF16)
            nm2 = sb("nm2", [128, 4], F32)
            nm2s = sb("nm2s", [128, 4], F32)
            lr2 = sb("lr2", [128, 4], F32)
            rc2 = sb("rc2", [128, 4], F32)
            negc = sb("negc", [128, 1], F32)
            b.memset("dve", negc[:], -25.0)

            def produce_qk(Q, R0, dst, dst_w, which, subs, off):
                with ExitStack() as pscope:
                    pq = [pscope.enter_context(
                        nc.psum_tensor(f"pq{pp}{Q}{dst_w}", [128, 512], F32))
                        for pp in range(3)]
                    pw = [pscope.enter_context(
                        nc.psum_tensor(f"pw{pp}{Q}{dst_w}", [128, 512], F32))
                        for pp in range(3)]
                    wtens = wb[which][0]
                    units = [(d, s0, w) for d in range(KC) for (s0, w) in subs]
                    nu = len(units)
                    Mc = [0] * nu
                    Vc = [0] * nu
                    Wc = [0] * nu
                    Rc = [0] * nu
                    b.wait("dve", "dve", b.count["dve"])
                    b.wait("pe", "dve", b.count["dve"])
                    for i in range(nu + 5):
                        u = i - 4
                        if 0 <= u < nu:
                            d, s0, w = units[u]
                            p2 = u % 2
                            p3 = u % 3
                            p4 = u % 4
                            b.wait("dve", "pe", Wc[u])
                            b.wait("dve", "act", Vc[u])
                            c = b.tt("dve", rt1[:, p2 * 512: p2 * 512 + w],
                                     qraw[:, p4 * 512: p4 * 512 + w],
                                     crep_sb[:, off + s0: off + s0 + w],
                                     ALU.mult)
                            b.wait("dve", "dve", c)
                            c = b.tt("dve", qraw[:, p4 * 512: p4 * 512 + w],
                                     pw[p3][:, 0:w],
                                     srep_sb[:, off + s0: off + s0 + w],
                                     ALU.mult)
                            b.wait("dve", "dve", c)
                            Rc[u] = b.tt("dve",
                                         dst[:, d * dst_w + s0:
                                             d * dst_w + s0 + w],
                                         rt1[:, p2 * 512: p2 * 512 + w],
                                         qraw[:, p4 * 512: p4 * 512 + w],
                                         ALU.add)
                        u = i - 1
                        if 0 <= u < nu:
                            d, s0, w = units[u]
                            p3 = u % 3
                            p4 = u % 4
                            b.wait("act", "pe", Mc[u])
                            if u >= 4:
                                b.wait("act", "pe", Wc[u - 4])
                                b.wait("act", "dve", Rc[u - 4])
                            Vc[u] = b.cp("act", qraw[:, p4 * 512: p4 * 512 + w],
                                         pq[p3][:, 0:w])
                        u = i
                        if 0 <= u < nu:
                            d, s0, w = units[u]
                            p3 = u % 3
                            if u >= 3:
                                b.wait("pe", "act", Vc[u - 3])
                            for k in range(KC):
                                Mc[u] = b.mm(
                                    pq[p3][:, 0:w],
                                    wtens[:, k * 1024 + d * 128:
                                          k * 1024 + (d + 1) * 128],
                                    xc2[:, k * R + off + s0:
                                        k * R + off + s0 + w],
                                    k == 0, k == KC - 1)
                        u = i - 3
                        if 0 <= u < nu:
                            d, s0, w = units[u]
                            p3 = u % 3
                            p4 = u % 4
                            b.wait("pe", "act", Vc[u])
                            if u >= 3:
                                b.wait("pe", "dve", Rc[u - 3])
                            Wc[u] = b.mm(pw[p3][:, 0:w], pswap_sb[:],
                                         qraw[:, p4 * 512: p4 * 512 + w],
                                         True, True)
                    b.wait("pe", "dve", b.count["dve"])
                    b.wait("pe", "act", b.count["act"])

            prev_outq = 0
            wout_pe = 0
            b.wait("pool", "dve", b.count["dve"])
            b.dma("pool", "dpl", crep_sb[:], crep_d[:, 0:R])
            b.dma("pool", "dpl", srep_sb[:], srep_d[:, 0:R])
            cs_pl = b.count["dpl"]
            lv = wload("A", 2, wqkv_d[:, 2048:3072])
            lq = wload("B", 0, wqkv_d[:, 0:1024])
            for Q in range(cfg.nq):
                R0 = Q * cfg.qown
                b.wait("dve", "dpl", cs_pl)
                emit_stats(f"a{Q}", R0, R, meanr[:], rstdr[:], None)
                with ExitStack() as pscope:
                    pb = pscope.enter_context(
                        nc.psum_tensor(f"pba{Q}", [128, 512], F32))
                    emit_cols(rstdr[:], nvt, rcolq, pb)
                    for (t0, w) in cfg.rsubs():
                        b.wait("pe", "dve", b.count["dve"])
                        b.wait("pe", "act", b.count["act"])
                        mmc = b.mm(pb[:, 0:w], ones1[:],
                                   rstdr[0:1, t0:t0 + w], True, True)
                        b.wait("dve", "pe", mmc)
                        c = b.tt("dve", crep_sb[:, t0:t0 + w],
                                 crep_sb[:, t0:t0 + w], pb[:, 0:w], ALU.mult)
                        b.wait("dve", "dve", c)
                        c = b.tt("dve", srep_sb[:, t0:t0 + w],
                                 srep_sb[:, t0:t0 + w], pb[:, 0:w], ALU.mult)
                        b.wait("pe", "dve", c)
                    for (t0, w) in cfg.rsubs():
                        b.wait("pe", "dve", b.count["dve"])
                        mmc = b.mm(pb[:, 0:w], ones1[:], meanr[0:1, t0:t0 + w],
                                   True, True)
                        b.wait("dve", "pe", mmc)
                        ev = 0
                        for k in range(KC):
                            ev = b.tt("dve",
                                      xc2[:, k * R + t0: k * R + t0 + w],
                                      x_sb[:, k * T + R0 + t0:
                                           k * T + R0 + t0 + w],
                                      pb[:, 0:w], ALU.subtract)
                        b.wait("pe", "dve", ev)
                if Q > 0:
                    b.wait("dve", "dve", b.count["dve"])
                    p0 = Q * cfg.qown
                    for d in range(KC):
                        b.tt("dve", x_sb[:, d * T + p0: d * T + p0 + 128],
                             x_sb[:, d * T + p0: d * T + p0 + 128],
                             outq[:, d * 128:(d + 1) * 128], ALU.add)

                with ExitStack() as pscope:
                    pv = pscope.enter_context(
                        nc.psum_tensor(f"pv{Q}", [128, 1024], F32))
                    ev_sl = [0, 0]
                    b.wait("pe", "dve", b.count["dve"])
                    b.wait("dve", "dve", b.count["dve"])
                    wwait("A", lv)
                    for tt in range(nvt):
                        for vc in range(2):
                            sl = (tt * 2 + vc) % 2
                            if ev_sl[sl]:
                                b.wait("pe", "dve", ev_sl[sl])
                            for k in range(KC):
                                mmc = b.mm(
                                    pv[:, sl * 512: sl * 512 + 512],
                                    xc2[:, k * R + tt * 128:
                                        k * R + (tt + 1) * 128],
                                    wbufA[:, k * 1024 + vc * 512:
                                          k * 1024 + (vc + 1) * 512],
                                    k == 0, k == KC - 1)
                            b.wait("dve", "pe", mmc)
                            ev_sl[sl] = b.ts(
                                "dve",
                                v_sb[:, tt * 1024 + vc * 512:
                                     tt * 1024 + (vc + 1) * 512],
                                pv[:, sl * 512: sl * 512 + 512],
                                rcolq[:, tt: tt + 1], ALU.mult)
                    b.wait("pe", "dve", max(ev_sl))
                    wdone("A")
                lk = wload("A", 1, wqkv_d[:, 1024:2048])

                wwait("B", lq)
                qsubs = [(s, min(512, cfg.qown - s))
                         for s in range(0, cfg.qown, 512)]
                produce_qk(Q, R0, q_sb, cfg.qown, "B", qsubs, 128)
                wdone("B")
                lo = wload("B", 3, wout_d[:, :])
                wwait("A", lk)
                produce_qk(Q, R0, k_sb, R, "A", cfg.rsubs(), 0)
                wdone("A")
                if Q + 1 < cfg.nq:
                    b.wait("pool", "dve", b.count["dve"])
                    R0n = (Q + 1) * cfg.qown
                    b.dma("pool", "dpl", crep_sb[:], crep_d[:, R0n:R0n + R])
                    b.dma("pool", "dpl", srep_sb[:], srep_d[:, R0n:R0n + R])
                    cs_pl = b.count["dpl"]
                    lv = wload("A", 2, wqkv_d[:, 2048:3072])

                wwait("B", lo)
                with ExitStack() as pscope:
                    ps_s = pscope.enter_context(
                        nc.psum_tensor(f"pss{Q}", [128, 2048], F32))
                    ps_pt = [pscope.enter_context(
                        nc.psum_tensor(f"pspt{pp}{Q}", [128, 768], BF16))
                        for pp in range(2)]
                    ps_o = [pscope.enter_context(
                        nc.psum_tensor(f"pso{pp}{Q}", [128, 128], F32))
                        for pp in range(2)]
                    n = cfg.qpq * 8
                    slots = [(qi, hp) for qi in range(cfg.qpq)
                             for hp in range(8)]
                    Acnt = [0] * n
                    Bcnt = [0] * n
                    Ccnt = [0] * n
                    Dcnt = [0] * n
                    Ecnt = [0] * n
                    Fcnt = [0] * n
                    Gcnt = [0] * n
                    Hcnt = [0] * n
                    Xd = [0] * n

                    def bsel_of(j):
                        gqt = Q * cfg.qpq + slots[j][0]
                        return 1 if gqt == 0 else (
                            2 if gqt == cfg.last_qt else 0)

                    b.wait("pe", "dve", b.count["dve"])
                    for i in range(n + 8):
                        j = i - 2
                        if 0 <= j < n:
                            pr = j % 2
                            b.wait("act", "pe", Acnt[j])
                            if j >= 2:
                                b.wait("act", "dve", Dcnt[j - 2])
                            for jh in range(2):
                                Ccnt[j] = b.actv(
                                    p32[:, pr * 768 + jh * 384:
                                        pr * 768 + (jh + 1) * 384],
                                    ps_s[:, (2 * pr + jh) * 512:
                                         (2 * pr + jh) * 512 + 384],
                                    AF.Exp, bias=negc[:, 0:1],
                                    scale=0.125,
                                    accum=lr2[:, 2 * pr + jh:2 * pr + jh + 1])
                        if i < n:
                            qi, hp = slots[i]
                            bsel = bsel_of(i)
                            if i >= 2:
                                b.wait("pe", "act", Ccnt[i - 2])
                            for jh in range(2):
                                hb = jh * 64
                                sb2 = ((i % 2) * 2 + jh) * 512
                                b.mm(ps_s[:, sb2: sb2 + 384], idb[:],
                                     biasq_sb[:, bsel * 384:(bsel + 1) * 384],
                                     True, False)
                                Acnt[i] = b.mm(
                                    ps_s[:, sb2: sb2 + 384],
                                    q_sb[hb:hb + 64, hp * cfg.qown + qi * 128:
                                         hp * cfg.qown + (qi + 1) * 128],
                                    k_sb[hb:hb + 64, hp * R + qi * 128:
                                         hp * R + qi * 128 + 384],
                                    False, True)
                        j = i - 3
                        if 0 <= j < n:
                            pr = j % 2
                            b.wait("dve", "act", Ccnt[j])
                            c = b.recip(rc2[:, 2 * pr:2 * pr + 2],
                                        lr2[:, 2 * pr:2 * pr + 2])
                            b.wait("dve", "dve", c)
                            if j >= 2:
                                b.wait("dve", "pe", Ecnt[j - 2])
                            for jh in range(2):
                                Dcnt[j] = b.ts(
                                    "dve",
                                    pn_bf[:, pr * 768 + jh * 384:
                                          pr * 768 + (jh + 1) * 384],
                                    p32[:, pr * 768 + jh * 384:
                                        pr * 768 + (jh + 1) * 384],
                                    rc2[:, 2 * pr + jh:2 * pr + jh + 1],
                                    ALU.mult)
                        j = i - 4
                        if 0 <= j < n:
                            pr = j % 2
                            b.wait("pe", "dve", Dcnt[j])
                            if j >= 2:
                                b.wait("pe", "act", Fcnt[j - 2])
                            for kb in range(6):
                                Ecnt[j] = b.tr(
                                    ps_pt[pr][:, kb * 128:(kb + 1) * 128],
                                    pn_bf[:, pr * 768 + kb * 128:
                                          pr * 768 + (kb + 1) * 128],
                                    idb[:])
                        j = i - 5
                        if 0 <= j < n:
                            pr = j % 2
                            b.wait("act", "pe", Ecnt[j])
                            if j >= 2:
                                b.wait("act", "pe", Gcnt[j - 2])
                            Fcnt[j] = b.cp("act",
                                           pt_sb[:, pr * 768:(pr + 1) * 768],
                                           ps_pt[pr][:])
                        j = i - 6
                        if 0 <= j < n:
                            pr = j % 2
                            qi1, hp1 = slots[j]
                            b.wait("pe", "act", Fcnt[j])
                            if j >= 2:
                                b.wait("pe", "act", Hcnt[j - 2])
                            for jh in range(2):
                                h1 = hp1 * 2 + jh
                                hb1 = jh * 64
                                for kb in range(3):
                                    Gcnt[j] = b.mm(
                                        ps_o[pr][hb1:hb1 + 64, :],
                                        v_sb[:, (qi1 + kb) * 1024 + h1 * 64:
                                             (qi1 + kb) * 1024 +
                                             (h1 + 1) * 64],
                                        pt_sb[:, pr * 768 + jh * 384 +
                                              kb * 128:
                                              pr * 768 + jh * 384 +
                                              (kb + 1) * 128],
                                        kb == 0, kb == 2,
                                        tile_position=(0, hb1))
                        j = i - 7
                        if 0 <= j < n:
                            pr = j % 2
                            qi1, hp1 = slots[j]
                            b.wait("act", "pe", Gcnt[j])
                            if wout_pe:
                                b.wait("act", "pe", wout_pe)
                            Hcnt[j] = b.cp(
                                "act",
                                attnq4[:, hp1 * qtw + qi1 * 128:
                                       hp1 * qtw + (qi1 + 1) * 128],
                                ps_o[pr][:])
                    b.wait("pe", "act", b.count["act"])
                    wev = {0: 0, 1: 0}
                    for dout in range(KC):
                        wp = dout % 2
                        if wev[wp]:
                            b.wait("pe", "dve", wev[wp])
                        for k in range(KC):
                            mmc = b.mm(ps_s[:, wp * 1024: wp * 1024 + qtw],
                                       wbufB[:, k * 1024 + dout * 128:
                                             k * 1024 + (dout + 1) * 128],
                                       attnq4[:, k * qtw:(k + 1) * qtw],
                                       k == 0, k == KC - 1)
                        b.wait("dve", "pe", mmc)
                        b.wait("dve", "dve", b.count["dve"])
                        abs0 = 128 + Q * cfg.qown
                        nlast = qtw - 128
                        if nlast > 0:
                            b.stt("dve",
                                  x_sb[:, dout * T + abs0:
                                       dout * T + abs0 + nlast],
                                  ps_s[:, wp * 1024: wp * 1024 + nlast],
                                  bout_sb[:, dout:dout + 1],
                                  x_sb[:, dout * T + abs0:
                                       dout * T + abs0 + nlast],
                                  ALU.add, ALU.add)
                        wev[wp] = b.ts(
                            "dve",
                            outq[:, dout * 128:(dout + 1) * 128],
                            ps_s[:, wp * 1024 + nlast:
                                 wp * 1024 + nlast + 128],
                            bout_sb[:, dout:dout + 1], ALU.add)
                    wout_pe = b.count["pe"]
                    wdone("B")
                if Q + 1 < cfg.nq:
                    lq = wload("B", 0, wqkv_d[:, 0:1024])
                prev_outq = b.count["dve"]
            b.wait("dve", "dve", b.count["dve"])
            p0 = cfg.nq * cfg.qown
            for d in range(KC):
                b.tt("dve", x_sb[:, d * T + p0: d * T + p0 + 128],
                     x_sb[:, d * T + p0: d * T + p0 + 128],
                     outq[:, d * 128:(d + 1) * 128], ALU.add)

    def barrier():
        for e in ENGINES:
            for s in ("pe", "dve", "act", "pool", "dsp", "dpl",
                      "w0", "w1", "w2", "w3",
                      "tee0", "tee1", "tee2", "tee3"):
                b.wait(e, s)

    if "ffn0" in phases:
        emit_ffn(0)
        barrier()
    if "attn" in phases:
        emit_attn()
        barrier()
    if "ffn1" in phases:
        emit_ffn(1)
        barrier()

    b.wait("sp", "dve", b.count["dve"])
    b.wait("sp", "pool", b.count["pool"])
    for d in range(KC):
        b.dma("sp", "dso", out_d[d * 128:(d + 1) * 128, :],
              x_sb[:, d * T + 128: d * T + 128 + cfg.own])
    b.wait("sp", "dso", b.count["dso"])

    with nc.Block() as block:
        @block.tensor
        def _(t):
            for fn in b.ops["pe"]:
                fn(t)

        @block.vector
        def _(v):
            for fn in b.ops["dve"]:
                fn(v)

        @block.scalar
        def _(s):
            for fn in b.ops["act"]:
                fn(s)

        @block.gpsimd
        def _(g):
            for fn in b.ops["pool"]:
                fn(g)

        @block.sync
        def _(s):
            for fn in b.ops["sp"]:
                fn(s)

    stack.close()
    return nc


def make_consts():
    ident = np.eye(128, dtype=np.float32)
    pswap = np.zeros((128, 128), dtype=np.float32)
    for j in range(128):
        i = (j + 32) % 64 + 64 * (j // 64)
        pswap[i, j] = 1.0
    return ident, pswap


def make_band_bias(left_edge, right_edge):
    i = np.arange(128)[:, None]
    j = np.arange(384)[None, :]
    band = np.where((j >= i) & (j <= i + 256), 0.0, NEG).astype(np.float32)
    first = band.copy()
    if left_edge:
        first[:, 0:128] = NEG
    last = band.copy()
    if right_edge:
        last[:, 256:384] = NEG
    out = np.ascontiguousarray(np.concatenate([band, first, last], axis=1))
    for v in range(3):
        assert np.all(out[:, v * 384 + 128: v * 384 + 256] == 0.0)
    return out


def make_rope(positions):
    inv = 1.0 / (10000.0 ** (np.arange(0, ROT * 2, 2, dtype=np.float32)
                             / (ROT * 2)))
    fr = positions[None, :].astype(np.float32) * inv[:, None]
    c = np.cos(fr).astype(np.float32)
    s = np.sin(fr).astype(np.float32)
    crep = np.ascontiguousarray(np.concatenate([c, c, c, c], axis=0))
    srep = np.ascontiguousarray(np.concatenate([-s, s, -s, s], axis=0))
    return crep, srep


def core_inputs(cfg: CFG, x_seq, seq_start, seq_len):
    own, halo, T = cfg.own, cfg.halo, cfg.T
    xe = np.zeros((T, D), dtype=np.float32)
    xe[halo:halo + own] = x_seq[seq_start:seq_start + own]
    left_edge = seq_start < halo
    if not left_edge:
        xe[0:halo] = x_seq[seq_start - halo:seq_start]
    right_edge = seq_start + own + halo > seq_len
    if not right_edge:
        xe[halo + own:] = x_seq[seq_start + own:seq_start + own + halo]
    xT = np.ascontiguousarray(xe.T)
    pos = np.arange(seq_start - halo, seq_start + own + halo)
    crep, srep = make_rope(pos)
    biasq = make_band_bias(left_edge, right_edge)
    return xT, crep, srep, biasq


def kernel(L0_attn_x, L0_attn_Wqkv, L0_attn_Wout, L0_attn_bout,
           L0_ffn_gate_w, L0_ffn_expert_w):
    cfg = CFG(own=2048, qown=512)
    x = np.asarray(L0_attn_x, dtype=np.float32)
    wqkv = np.ascontiguousarray(np.asarray(L0_attn_Wqkv, dtype=np.float32))
    wout = np.ascontiguousarray(np.asarray(L0_attn_Wout, dtype=np.float32))
    bout = np.asarray(L0_attn_bout, dtype=np.float32)
    gw = np.asarray(L0_ffn_gate_w, dtype=np.float32)
    wexp = np.asarray(L0_ffn_expert_w, dtype=np.float32)

    boutT = np.ascontiguousarray(bout.reshape(8, 128).T)
    gwT = np.ascontiguousarray(gw.T)
    wexpT = np.ascontiguousarray(wexp.transpose(0, 2, 1))
    ident, pswap = make_consts()

    B_, S, _ = x.shape
    n_cores = 8
    per_batch = n_cores // B_
    in_maps = []
    for c in range(n_cores):
        bidx, qidx = c // per_batch, c % per_batch
        xT, crep, srep, biasq = core_inputs(cfg, x[bidx], qidx * cfg.own, S)
        in_maps.append({
            "xT": xT, "wqkv": wqkv, "wout": wout, "boutT": boutT,
            "gwT": gwT, "wexpT": wexpT, "ident": ident,
            "pswap": pswap, "crep": crep, "srep": srep, "biasq": biasq,
        })

    nc = build_program(cfg)
    res = run_bass_kernel_spmd(nc, in_maps, core_ids=list(range(n_cores)))
    out = np.empty_like(x)
    for c in range(n_cores):
        bidx, qidx = c // per_batch, c % per_batch
        out[bidx, qidx * cfg.own:(qidx + 1) * cfg.own, :] = \
            res.results[c]["outT"].T
    return out


# version 48
# speedup vs baseline: 1.0028x; 1.0028x over previous
from contextlib import ExitStack

import numpy as np

import concourse.bass as bass
import concourse.mybir as mybir
from concourse.bass_utils import run_bass_kernel_spmd

F32 = mybir.dt.float32
BF16 = mybir.dt.bfloat16
AF = mybir.ActivationFunctionType
ALU = mybir.AluOpType
AX = mybir.AxisListType
NEG = -1.0e30

D = 1024
KC = 8
ROT = 32
EPS = 1e-5


class CFG:
    def __init__(self, own=2048, qown=512):
        self.own = own
        self.halo = 128
        self.T = own + 2 * self.halo
        self.qown = qown
        self.nq = own // qown
        self.R = qown + 256
        self.qpq = qown // 128
        self.ttiles = self.T // 128
        self.last_qt = own // 128 - 1

    def tchunks(self):
        out, t = [], 0
        while t < self.T:
            w = min(512, self.T - t)
            out.append((t, w))
            t += w
        if len(out) >= 2 and out[-1][1] < 512:
            tot = out[-2][1] + out[-1][1]
            w1 = tot // 2
            out[-2] = (out[-2][0], w1)
            out[-1] = (out[-2][0] + w1, tot - w1)
        return out

    def rsubs(self):
        out, t = [], 0
        while t < self.R:
            w = min(512, self.R - t)
            out.append((t, w))
            t += w
        return out


ENGINES = ("pe", "dve", "act", "pool", "sp")


class B:

    def __init__(self, nc, stack):
        self.nc = nc
        self.ops = {e: [] for e in ENGINES}
        self.sems = {}
        self.count = {}
        self.hw = {}
        self.stack = stack

    def sem(self, name):
        self.sems[name] = self.stack.enter_context(self.nc.semaphore(name))
        self.count[name] = 0

    def emit(self, eng, fn, inc=None, n=1):
        if inc is None:
            self.ops[eng].append(fn)
            return 0
        self.count[inc] += n
        s = self.sems[inc]
        self.ops[eng].append(lambda E, fn=fn, s=s, n=n: fn(E).then_inc(s, n))
        return self.count[inc]

    def wait(self, eng, sem, val=None):
        if val is None:
            val = self.count[sem]
        if val <= 0:
            return
        key = (eng, sem)
        if self.hw.get(key, 0) >= val:
            return
        self.hw[key] = val
        s = self.sems[sem]
        self.ops[eng].append(lambda E, s=s, v=val: E.wait_ge(s, v))

    def mm(self, ps, lt, rs, st, sp, tile_position=None):
        return self.emit(
            "pe",
            lambda E, a=ps, b=lt, c=rs, st=st, sp=sp, tp=tile_position:
                E.matmul(a, b, c, start=st, stop=sp, tile_position=tp),
            inc="pe")

    def tr(self, ps, in_, ident):
        return self.emit(
            "pe", lambda E, a=ps, b=in_, c=ident: E.transpose(a, b, c),
            inc="pe")

    def tt(self, eng, out, a0, a1, op):
        inc = "dve" if eng == "dve" else "pool"
        return self.emit(
            eng, lambda E, o=out, x=a0, y=a1, op=op: E.tensor_tensor(o, x, y, op),
            inc=inc)

    def ts(self, eng, out, a0, s1, op):
        inc = "dve" if eng == "dve" else "pool"
        return self.emit(
            eng,
            lambda E, o=out, x=a0, s=s1, op=op: E.tensor_scalar(o, x, s, None, op),
            inc=inc)

    def stt(self, eng, out, a0, s, a1, op0, op1):
        inc = {"dve": "dve", "act": "act"}.get(eng, "pool")
        return self.emit(
            eng,
            lambda E, o=out, x=a0, s=s, y=a1, p=op0, q=op1:
                E.scalar_tensor_tensor(o, x, s, y, p, q),
            inc=inc)

    def cp(self, eng, out, in_):
        if eng == "act":
            return self.emit(
                "act", lambda E, o=out, i=in_: E.copy(o, i), inc="act")
        inc = "dve" if eng == "dve" else "pool"
        return self.emit(
            eng, lambda E, o=out, i=in_: E.tensor_copy(o, i), inc=inc)

    def red(self, out, in_, op=ALU.max, negate=False):
        return self.emit(
            "dve",
            lambda E, o=out, i=in_, op=op, ng=negate:
                E.tensor_reduce(o, i, AX.X, op, negate=ng),
            inc="dve")

    def actv(self, out, in_, func, bias=0.0, scale=1.0, accum=None):
        return self.emit(
            "act",
            lambda E, o=out, i=in_, f=func, b=bias, s=scale, a=accum:
                E.activation(o, i, f, bias=b, scale=s, accum_out=a),
            inc="act")

    def recip(self, out, in_):
        return self.emit(
            "dve", lambda E, o=out, i=in_: E.reciprocal(o, i), inc="dve")

    def dma(self, eng, sem, out, in_):
        return self.emit(
            eng, lambda E, o=out, i=in_: E.dma_start(o, i), inc=sem, n=16)

    def memset(self, eng, ap, val):
        inc = "dve" if eng == "dve" else "pool"
        return self.emit(
            eng, lambda E, a=ap, v=val: E.memset(a, v), inc=inc)


def build_program(cfg: CFG, phases=("ffn0", "attn", "ffn1")):
    nc = bass.Bass("TRN2", target_bir_lowering=False)
    T, R = cfg.T, cfg.R

    xT_d = nc.dram_tensor("xT", [D, T], F32, kind="ExternalInput").ap()
    wqkv_d = nc.dram_tensor("wqkv", [D, 3 * D], F32, kind="ExternalInput").ap()
    wout_d = nc.dram_tensor("wout", [D, D], F32, kind="ExternalInput").ap()
    bout_d = nc.dram_tensor("boutT", [128, 8], F32, kind="ExternalInput").ap()
    gwT_d = nc.dram_tensor("gwT", [D, 8], F32, kind="ExternalInput").ap()
    wexp_d = nc.dram_tensor("wexpT", [8, D, D], F32, kind="ExternalInput").ap()
    ident_d = nc.dram_tensor("ident", [128, 128], F32, kind="ExternalInput").ap()
    pswap_d = nc.dram_tensor("pswap", [128, 128], F32, kind="ExternalInput").ap()
    crep_d = nc.dram_tensor("crep", [128, T], F32, kind="ExternalInput").ap()
    srep_d = nc.dram_tensor("srep", [128, T], F32, kind="ExternalInput").ap()
    biasq_d = nc.dram_tensor("biasq", [128, 3 * 384], F32, kind="ExternalInput").ap()
    out_d = nc.dram_tensor("outT", [D, cfg.own], F32, kind="ExternalOutput").ap()
    webf_d = nc.dram_tensor("wexp_bf", [8, 128, KC * D], BF16,
                            kind="Internal").ap()
    wqbf_d = nc.dram_tensor("wqkv_bf", [4, 128, KC * D], BF16,
                            kind="Internal").ap()
    oht_d = nc.dram_tensor("onehot_sc", [8, T], BF16, kind="Internal").ap()

    stack = ExitStack()
    b = B(nc, stack)
    for s in ("pe", "dve", "act", "pool", "dsp", "dpl", "w0", "w1", "w2",
              "w3", "tee0", "tee1", "tee2", "tee3", "mk0", "mk1", "dso",
              "dx0", "dx1", "dx2", "dx3", "dx4", "dx5", "dx6", "dx7"):
        b.sem(s)

    def sbuf(name, shape, dt):
        return stack.enter_context(nc.sbuf_tensor(name, shape, dt))

    x_sb = sbuf("x_sb", [128, KC * T], F32)
    bout_sb = sbuf("bout_sb", [128, 8], F32)
    gwT_sb = sbuf("gwT_sb", [128, KC * 8], F32)
    idb = sbuf("idb", [128, 128], BF16)
    pswap_sb = sbuf("pswap_sb", [128, 128], BF16)
    biasq_sb = sbuf("biasq_sb", [128, 3 * 384], BF16)
    ones_bf = sbuf("ones_bf", [128, 1], BF16)
    onescol = sbuf("onescol", [128, 1], F32)
    ones1 = sbuf("ones1", [1, 128], F32)
    xb_g = sbuf("xb_g", [128, 2048], BF16)
    xq_g = sbuf("xq_g", [128, 2048], BF16)
    rowtmp = sbuf("rowtmp", [1, T], F32)

    for k in range(KC):
        b.dma("sp", f"dx{k}", x_sb[:, k * T:(k + 1) * T],
              xT_d[k * 128:(k + 1) * 128, :])
    b.dma("sp", "dsp", bout_sb[:], bout_d[:])
    for k in range(KC):
        b.dma("sp", "dsp", gwT_sb[:, k * 8:(k + 1) * 8], gwT_d[k * 128:(k + 1) * 128, :])
    init_sp = b.count["dsp"]
    b.dma("pool", "dpl", idb[:], ident_d[:])
    b.dma("pool", "dpl", pswap_sb[:], pswap_d[:])
    b.dma("pool", "dpl", biasq_sb[:], biasq_d[:])
    init_pl = b.count["dpl"]
    b.memset("dve", ones_bf[:], 1.0)
    b.memset("dve", onescol[:], 1.0)
    b.memset("dve", ones1[:], 1.0)
    init_dve = b.count["dve"]

    def emit_stats(tag, a, L, meanr, rstdr, rmr):
        b.wait("dve", "dve", b.count["dve"])
        b.wait("act", "dve", b.count["dve"])
        with ExitStack() as pscope:
            p1 = pscope.enter_context(nc.psum_tensor(f"p1{tag}", [1, 512], F32))
            p2 = pscope.enter_context(nc.psum_tensor(f"p2{tag}", [1, 512], F32))
            t = 0
            pe_slot = {i: 0 for i in range(4)}
            while t < L:
                w = min(512, L - t)
                for k in range(KC):
                    sl = k % 4
                    if pe_slot[sl]:
                        b.wait("dve", "pe", pe_slot[sl])
                        b.wait("act", "pe", pe_slot[sl])
                    b.wait("dve", f"dx{k}", 16)
                    b.wait("act", f"dx{k}", 16)
                    cast_c = b.cp("dve", xb_g[:, sl * 512: sl * 512 + w],
                                  x_sb[:, k * T + a + t: k * T + a + t + w])
                    sq_c = b.actv(xq_g[:, sl * 512: sl * 512 + w],
                                  x_sb[:, k * T + a + t: k * T + a + t + w],
                                  AF.Square)
                    b.wait("pe", "dve", cast_c)
                    b.wait("pe", "act", sq_c)
                    b.mm(p1[:, 0:w], ones_bf[:], xb_g[:, sl * 512: sl * 512 + w],
                         k == 0, k == KC - 1)
                    pe_slot[sl] = b.mm(
                        p2[:, 0:w], ones_bf[:], xq_g[:, sl * 512: sl * 512 + w],
                        k == 0, k == KC - 1)
                b.wait("dve", "pe", b.count["pe"])
                b.cp("dve", meanr[0:1, t:t + w], p1[:, 0:w])
                ev = b.cp("dve", rstdr[0:1, t:t + w], p2[:, 0:w])
                b.wait("pe", "dve", ev)
                t += w
        b.wait("act", "dve", b.count["dve"])
        c = b.actv(meanr, meanr, AF.Copy, scale=1.0 / D)
        c = b.actv(rstdr, rstdr, AF.Copy, scale=1.0 / D)
        tmp = rowtmp[0:1, 0:L]
        b.wait("act", "act", c)
        c = b.actv(tmp, meanr, AF.Square)
        b.wait("dve", "act", c)
        c = b.tt("dve", rstdr, rstdr, tmp, ALU.subtract)
        b.wait("dve", "dve", c)
        c = b.ts("dve", rstdr, rstdr, EPS, ALU.add)
        b.wait("act", "dve", c)
        c = b.actv(tmp, rstdr, AF.Ln)
        b.wait("act", "act", c)
        c = b.actv(rstdr, tmp, AF.Exp, scale=-0.5)
        if rmr is not None:
            b.wait("dve", "act", c)
            b.tt("dve", rmr, meanr, rstdr, ALU.mult)

    def emit_bcast(row, L, dst, pbank):
        t = 0
        while t < L:
            w = min(512, L - t)
            b.wait("pe", "dve", b.count["dve"])
            b.wait("pe", "act", b.count["act"])
            mmc = b.mm(pbank[:, 0:w], ones1[:], row[0:1, t:t + w], True, True)
            b.wait("dve", "pe", mmc)
            ev = b.cp("dve", dst[:, t:t + w], pbank[:, 0:w])
            b.wait("pe", "dve", ev)
            t += w

    def emit_cols(row, ntiles, dst, pbank):
        b.wait("pe", "dve", b.count["dve"])
        b.wait("pe", "act", b.count["act"])
        for tt in range(ntiles):
            b.mm(pbank[:, tt: tt + 1],
                 row[0:1, tt * 128:(tt + 1) * 128], ones1[0:1, 0:1],
                 True, True)
        b.wait("dve", "pe", b.count["pe"])
        ev = b.cp("dve", dst[:, 0:ntiles], pbank[:, 0:ntiles])
        b.wait("pe", "dve", ev)

    def emit_ffn(ffn_idx):
        tag = f"f{ffn_idx}"
        with ExitStack() as fs:
            sb = lambda n, sh, dt: fs.enter_context(
                nc.sbuf_tensor(n + tag, sh, dt))
            meanr = sb("meanr", [1, T], F32)
            rstdr = sb("rstdr", [1, T], F32)
            rmr = rowtmp[0:1, 0:T]
            emit_stats(tag, 0, T, meanr[:], rstdr[:], rmr)

            hrb_sb = sb("hrb", [128, T], F32)
            mb_sb = sb("mb", [128, T], F32)
            rcol = sb("rcol", [128, cfg.ttiles], F32)
            rmcol = sb("rmcol", [128, cfg.ttiles], F32)

            onehotT = sb("onehotT", [8, T], BF16)
            g1T = sb("g1T", [128, 8], F32)
            g1r = sb("g1r", [1, 8], F32)
            nt = cfg.ttiles
            ohbuf = sb("ohbuf", [128, 8 * nt], BF16)
            u144 = sb("u144", [128, 8 * nt], F32)
            v144 = sb("v144", [128, 8 * nt], F32)
            c144 = sb("c144", [128, 8 * nt], F32)
            m18 = sb("m18", [128, nt], F32)
            with ExitStack() as pscope:
                pg1 = pscope.enter_context(
                    nc.psum_tensor("pg1" + tag, [1, 8], F32))
                pgb = pscope.enter_context(
                    nc.psum_tensor("pgb" + tag, [128, 8], F32))
                pg = pscope.enter_context(
                    nc.psum_tensor("pg" + tag, [128, 8 * nt], F32))
                pot = [pscope.enter_context(
                    nc.psum_tensor(f"pot{pp}" + tag, [8, 128], BF16))
                    for pp in range(2)]
                b.wait("pe", "dsp", init_sp)
                b.wait("pe", "dve", init_dve)
                for k in range(KC):
                    b.mm(pg1[:], onescol[:], gwT_sb[:, k * 8:(k + 1) * 8],
                         k == 0, k == KC - 1)
                b.wait("dve", "pe", b.count["pe"])
                ev = b.cp("dve", g1r[:], pg1[:])
                b.wait("pe", "dve", ev)
                mmc = b.mm(pgb[:], ones1[:], g1r[:], True, True)
                b.wait("dve", "pe", mmc)
                b.cp("dve", g1T[:], pgb[:])
                for tt in range(nt):
                    for k in range(KC):
                        b.wait("pe", f"dx{k}", 16)
                        b.mm(pg[:, tt * 8: tt * 8 + 8],
                             x_sb[:, k * T + tt * 128: k * T + (tt + 1) * 128],
                             gwT_sb[:, k * 8:(k + 1) * 8],
                             k == 0, k == KC - 1)
                gate_pe = b.count["pe"]
                with ExitStack() as bscope:
                    pb = bscope.enter_context(
                        nc.psum_tensor("pb" + tag, [128, 512], F32))
                    emit_cols(rstdr[:], cfg.ttiles, rcol, pb)
                    emit_cols(rmr, cfg.ttiles, rmcol, pb)
                    emit_bcast(meanr[:], T, mb_sb, pb)
                    b.wait("dve", "act", b.count["act"])
                    c = b.ts("dve", rstdr[:], rstdr[:], 0.5, ALU.mult)
                    b.wait("dve", "dve", c)
                    emit_bcast(rstdr[:], T, hrb_sb, pb)
                b.wait("dve", "pe", gate_pe)
                b.wait("dve", "dve", b.count["dve"])
                pg3 = pg[:].rearrange("p (t e) -> p t e", e=8)
                u3 = u144[:].rearrange("p (t e) -> p t e", e=8)
                v3 = v144[:].rearrange("p (t e) -> p t e", e=8)
                c3 = c144[:].rearrange("p (t e) -> p t e", e=8)
                oh3 = ohbuf[:].rearrange("p (t e) -> p t e", e=8)
                rcol_b = rcol[:, 0:nt].unsqueeze(2).broadcast_to((128, nt, 8))
                rmcol_b = rmcol[:, 0:nt].unsqueeze(2).broadcast_to((128, nt, 8))
                g1_b = g1T[:].unsqueeze(1).broadcast_to((128, nt, 8))
                m_b = m18[:, 0:nt].unsqueeze(2).broadcast_to((128, nt, 8))
                c = b.tt("dve", u3, pg3, rcol_b, ALU.mult)
                b.wait("dve", "dve", c)
                c = b.tt("dve", v3, g1_b, rmcol_b, ALU.mult)
                b.wait("dve", "dve", c)
                c = b.tt("dve", u3, u3, v3, ALU.subtract)
                b.wait("dve", "dve", c)
                c = b.red(m18[:, 0:nt].unsqueeze(2), u3)
                b.wait("dve", "dve", c)
                c = b.tt("dve", v3, u3, m_b, ALU.is_equal)
                b.wait("dve", "dve", c)
                c = b.cp("dve", c3[:, :, 0:1], v3[:, :, 0:1])
                for e in range(1, 8):
                    b.wait("dve", "dve", c)
                    c = b.tt("dve", c3[:, :, e:e + 1], c3[:, :, e - 1:e],
                             v3[:, :, e:e + 1], ALU.add)
                b.wait("dve", "dve", c)
                c = b.ts("dve", u3, c3, 1.0, ALU.is_equal)
                b.wait("dve", "dve", c)
                c = b.tt("dve", oh3, v3, u3, ALU.mult)
                b.wait("dve", "dve", c)
                b.wait("pe", "dve", b.count["dve"])
                b.wait("pe", "dpl", init_pl)
                last_ev = {0: 0, 1: 0}
                for tt in range(nt):
                    sl = tt % 2
                    if last_ev[sl]:
                        b.wait("pe", "dve", last_ev[sl])
                    trc = b.tr(pot[sl][:],
                               ohbuf[:, tt * 8: tt * 8 + 8], idb[:])
                    b.wait("dve", "pe", trc)
                    last_ev[sl] = b.cp(
                        "dve", onehotT[:, tt * 128:(tt + 1) * 128],
                        pot[sl][:])

            b.wait("sp", "dve", b.count["dve"])
            oh_w = b.dma("sp", "dsp", oht_d[:, :], onehotT[:, :])

            xc = sb("xc", [128, 2 * KC * 512], BF16)
            mkbuf = sb("mkbuf", [128, 2 * 8 * 512], BF16)
            webuf = sb("webuf", [128, 4 * 2048], BF16)
            lmbuf = sb("lmbuf", [128, 4 * 1024], BF16)
            ev8 = sb("ev8", [128, KC * 512], BF16)
            uscr = sb("uscr", [128, 2 * 512], F32)
            pd = fs.enter_context(nc.psum_tensor("pd" + tag, [128, 4096], F32))
            wsem = ["w0", "w1", "w2", "w3"]
            tsem = ["tee0", "tee1", "tee2", "tee3"]
            wload = [b.count[s] // 16 for s in wsem]
            wfree = [0, 0, 0, 0]
            wtee = [0, 0, 0, 0]
            lmfree = [0, 0, 0, 0]
            sidx = 0
            piece_tee = [0] * 32
            tcs = cfg.tchunks()

            x3 = x_sb[:].rearrange("p (k t) -> p k t", t=T)
            pd3 = pd[:].rearrange("p (d f) -> p d f", f=512)
            ev3 = ev8[:].rearrange("p (d f) -> p d f", f=512)
            mkload = [0, 0]

            def xc3(par):
                return xc[:, par * KC * 512:(par + 1) * KC * 512].rearrange(
                    "p (k f) -> p k f", f=512)

            def gen_xc(ti, t0, w, after=0):
                b.wait("dve", "dve", max(after, b.count["dve"]))
                return b.tt(
                    "dve", xc3(ti % 2)[:, :, 0:w], x3[:, :, t0:t0 + w],
                    mb_sb[:, t0:t0 + w].unsqueeze(1).broadcast_to((128, KC, w)),
                    ALU.subtract)

            def load_masks(ti, t0, w):
                mp = ti % 2
                msem = "mk0" if mp == 0 else "mk1"
                b.wait("sp", "dsp", oh_w)
                for e in range(8):
                    mkload[mp] = b.dma(
                        "sp", msem,
                        mkbuf[:, mp * 4096 + e * 512: mp * 4096 + e * 512 + w],
                        oht_d[e:e + 1, t0:t0 + w].broadcast_to((128, w)))
                return mkload[mp]

            def emit_phase2_d(d, t0, w, p1cnt, p1acnt):
                eng = "dve" if d < 4 else "pool"
                sl_ = 0 if d < 4 else 1
                us = uscr[:, sl_ * 512: sl_ * 512 + w]
                b.wait(eng, "dve", p1cnt)
                b.wait(eng, "act", p1acnt)
                b.wait(eng, eng, b.count[eng])
                cc = b.tt(eng, us, x_sb[:, d * T + t0: d * T + t0 + w],
                          mb_sb[:, t0:t0 + w], ALU.subtract)
                b.wait(eng, eng, cc)
                cc = b.tt(eng, us, us, ev8[:, d * 512: d * 512 + w], ALU.add)
                b.wait(eng, eng, cc)
                cc = b.tt(eng, us, us, hrb_sb[:, t0:t0 + w], ALU.mult)
                b.wait(eng, eng, cc)
                cc = b.tt(eng, x_sb[:, d * T + t0: d * T + t0 + w],
                          x_sb[:, d * T + t0: d * T + t0 + w],
                          us, ALU.add)
                return cc, eng

            xc_dve = gen_xc(0, tcs[0][0], tcs[0][1])
            mk_w = load_masks(0, tcs[0][0], tcs[0][1])
            xc_next = xc_dve
            mk_next = mk_w
            lm_last = [0, 0]
            pending = []
            drain_w = {}
            ph2 = {"dve": 0, "pool": 0}
            for ti, (t0, w) in enumerate(tcs):
                xc_dve, mk_w = xc_next, mk_next
                mp = ti % 2
                msem = "mk0" if mp == 0 else "mk1"
                b.wait("pe", "dve", max(xc_dve, ph2["dve"]))
                b.wait("pe", "pool", ph2["pool"])
                for e in range(8):
                    if pending:
                        cc, eng = pending.pop(0)()
                        ph2[eng] = cc
                    if e == 5 and ti + 1 < len(tcs):
                        b.wait("sp", "dve", lm_last[(ti + 1) % 2])
                        mk_next = load_masks(ti + 1, tcs[ti + 1][0],
                                             tcs[ti + 1][1])
                    if e == 6 and ti + 1 < len(tcs):
                        xc_next = gen_xc(ti + 1, tcs[ti + 1][0], tcs[ti + 1][1])
                    for h in range(4):
                        buf = sidx % 4
                        if wfree[buf]:
                            b.wait("pool", "pe", wfree[buf])
                        if wtee[buf]:
                            b.wait("pool", tsem[buf], wtee[buf])
                        if ffn_idx == 0 and ti == 0:
                            dst = webuf[:, buf * 2048:
                                        (buf + 1) * 2048].rearrange(
                                "p (k f) -> p k f", f=1024)
                            src = wexp_d[e, h * 256:(h + 1) * 256, :].rearrange(
                                "(k p) f -> p k f", p=128)
                        else:
                            piece = e * 4 + h
                            if ffn_idx == 0:
                                b.wait("pool", tsem[piece % 4],
                                       piece_tee[piece])
                            else:
                                for ts_ in tsem:
                                    b.wait("pool", ts_, b.count[ts_])
                            dst = webuf[:, buf * 2048:(buf + 1) * 2048]
                            src = webf_d[e, :, h * 2048:(h + 1) * 2048]
                        wload[buf] += 1
                        b.dma("pool", wsem[buf], dst, src)
                        if ffn_idx == 0 and ti == 0:
                            b.wait("sp", wsem[buf], 16 * wload[buf])
                            wtee[buf] = b.dma(
                                "sp", tsem[buf],
                                webf_d[e, :, h * 2048:(h + 1) * 2048],
                                webuf[:, buf * 2048:(buf + 1) * 2048])
                            piece_tee[e * 4 + h] = wtee[buf]
                        b.wait("pe", wsem[buf], 16 * wload[buf])
                        sl2 = (e * 4 + h) % 4
                        if lmfree[sl2]:
                            b.wait("dve", "pe", lmfree[sl2])
                        b.wait("dve", "dve", xc_dve)
                        b.wait("dve", msem, mk_w)
                        lm3 = lmbuf[:, sl2 * 1024:(sl2 + 1) * 1024].rearrange(
                            "p (k f) -> p k f", f=512)
                        lmr = b.tt(
                            "dve", lm3[:, :, 0:w],
                            xc3(mp)[:, 2 * h:2 * h + 2, 0:w],
                            mkbuf[:, mp * 4096 + e * 512:
                                  mp * 4096 + e * 512 + w]
                            .unsqueeze(1).broadcast_to((128, 2, w)),
                            ALU.mult)
                        b.wait("pe", "dve", lmr)
                        pec = 0
                        for kk in range(2):
                            k = h * 2 + kk
                            first = (e == 0 and k == 0)
                            last = (e == 7 and k == 7)
                            for d in range(KC):
                                if first and d in drain_w:
                                    eng2, c2 = drain_w.pop(d)
                                    b.wait("pe", eng2, c2)
                                pec = b.mm(
                                    pd[:, d * 512: d * 512 + w],
                                    webuf[:, buf * 2048 + kk * 1024 + d * 128:
                                          buf * 2048 + kk * 1024 + (d + 1) * 128],
                                    lmbuf[:, sl2 * 1024 + kk * 512:
                                          sl2 * 1024 + kk * 512 + w],
                                    first, last)
                        lmfree[sl2] = pec
                        wfree[buf] = pec
                        sidx += 1
                lm_last[mp] = b.count["dve"]
                b.wait("dve", "pe", b.count["pe"])
                b.wait("act", "pe", b.count["pe"])
                p1d1 = b.cp("dve", ev3[:, 0:2, 0:w], pd3[:, 0:2, 0:w])
                p1 = b.cp("dve", ev3[:, 2:4, 0:w], pd3[:, 2:4, 0:w])
                p1a1 = b.cp("act", ev3[:, 4:6, 0:w], pd3[:, 4:6, 0:w])
                p1a = b.cp("act", ev3[:, 6:8, 0:w], pd3[:, 6:8, 0:w])
                drain_w.clear()
                drain_w.update({0: ("dve", p1d1), 2: ("dve", p1),
                                4: ("act", p1a1), 6: ("act", p1a)})
                pending = [
                    (lambda d=d, t0=t0, w=w, p1=p1, p1a=p1a:
                     emit_phase2_d(d, t0, w, p1, p1a))
                    for d in range(KC)]
                if ti + 1 == len(tcs):
                    for fn in pending:
                        cc, eng = fn()
                        ph2[eng] = cc
                    pending = []


    def emit_attn():
        with ExitStack() as ats:
            sb = lambda n, sh, dt: ats.enter_context(
                nc.sbuf_tensor(n + "_at", sh, dt))
            crep_sb = sb("crep", [128, R], BF16)
            srep_sb = sb("srep", [128, R], BF16)

            wbufA = sb("wbufA", [128, 8192], BF16)
            wbufB = sb("wbufB", [128, 8192], BF16)
            wb = {"A": [wbufA, "w0", 0, 0, 0, ""],
                  "B": [wbufB, "w1", 0, 0, 0, ""]}
            wb["A"][2] = b.count["w0"] // 16
            wb["B"][2] = b.count["w1"] // 16
            att_teed = [0, 0, 0, 0]

            def wload(which, m, src2d):
                ent = wb[which]
                if ent[3]:
                    b.wait("pool", "pe", ent[3])
                if ent[4]:
                    b.wait("pool", ent[5], ent[4])
                ent[2] += 1
                if att_teed[m]:
                    b.wait("pool", f"tee{m}", att_teed[m])
                    b.dma("pool", ent[1], ent[0][:], wqbf_d[m, :, :])
                else:
                    dst = ent[0][:].rearrange("p (k f) -> p k f", f=1024)
                    src = src2d.rearrange("(k p) f -> p k f", p=128)
                    b.dma("pool", ent[1], dst, src)
                    b.wait("sp", ent[1], 16 * ent[2])
                    att_teed[m] = b.dma("sp", f"tee{m}", wqbf_d[m, :, :],
                                        ent[0][:])
                    ent[4] = att_teed[m]
                    ent[5] = f"tee{m}"
                return 16 * ent[2]

            def wwait(which, v):
                b.wait("pe", wb[which][1], v)

            def wdone(which):
                wb[which][3] = b.count["pe"]

            nvt = R // 128
            k_sb = sb("k_sb", [128, KC * R], BF16)
            v_sb = sb("v_sb", [128, nvt * 1024], BF16)
            q_sb = sb("q_sb", [128, KC * cfg.qown], BF16)
            xc2 = sb("xc2", [128, KC * R], BF16)
            outq = sb("outq", [128, KC * 128], F32)
            qtw = cfg.qpq * 128
            attnq4 = sb("attnq4", [128, KC * qtw], BF16)
            qraw = sb("qraw", [128, 2048], BF16)
            rt1 = sb("rt1", [128, 1024], BF16)
            rt2 = qraw
            meanr = sb("ameanr", [1, R], F32)
            rstdr = sb("arstdr", [1, R], F32)
            rcolq = sb("rcolq", [128, nvt], F32)
            p32 = sb("p32", [128, 2 * 768], F32)
            pn_bf = sb("pn_bf", [128, 2 * 768], BF16)
            pt_sb = sb("pt_sb", [128, 2 * 768], BF16)
            nm2 = sb("nm2", [128, 4], F32)
            nm2s = sb("nm2s", [128, 4], F32)
            lr2 = sb("lr2", [128, 4], F32)
            rc2 = sb("rc2", [128, 4], F32)
            negc = sb("negc", [128, 1], F32)
            b.memset("dve", negc[:], -25.0)

            def produce_qk(Q, R0, dst, dst_w, which, subs, off):
                with ExitStack() as pscope:
                    pq = [pscope.enter_context(
                        nc.psum_tensor(f"pq{pp}{Q}{dst_w}", [128, 512], F32))
                        for pp in range(3)]
                    pw = [pscope.enter_context(
                        nc.psum_tensor(f"pw{pp}{Q}{dst_w}", [128, 512], F32))
                        for pp in range(3)]
                    wtens = wb[which][0]
                    units = [(d, s0, w) for d in range(KC) for (s0, w) in subs]
                    nu = len(units)
                    Mc = [0] * nu
                    Vc = [0] * nu
                    Wc = [0] * nu
                    Rc = [0] * nu
                    b.wait("dve", "dve", b.count["dve"])
                    b.wait("pe", "dve", b.count["dve"])
                    for i in range(nu + 5):
                        u = i - 4
                        if 0 <= u < nu:
                            d, s0, w = units[u]
                            p2 = u % 2
                            p3 = u % 3
                            p4 = u % 4
                            b.wait("dve", "pe", Wc[u])
                            b.wait("dve", "act", Vc[u])
                            c = b.tt("dve", rt1[:, p2 * 512: p2 * 512 + w],
                                     qraw[:, p4 * 512: p4 * 512 + w],
                                     crep_sb[:, off + s0: off + s0 + w],
                                     ALU.mult)
                            b.wait("dve", "dve", c)
                            c = b.tt("dve", qraw[:, p4 * 512: p4 * 512 + w],
                                     pw[p3][:, 0:w],
                                     srep_sb[:, off + s0: off + s0 + w],
                                     ALU.mult)
                            b.wait("dve", "dve", c)
                            Rc[u] = b.tt("dve",
                                         dst[:, d * dst_w + s0:
                                             d * dst_w + s0 + w],
                                         rt1[:, p2 * 512: p2 * 512 + w],
                                         qraw[:, p4 * 512: p4 * 512 + w],
                                         ALU.add)
                        u = i - 1
                        if 0 <= u < nu:
                            d, s0, w = units[u]
                            p3 = u % 3
                            p4 = u % 4
                            b.wait("act", "pe", Mc[u])
                            if u >= 4:
                                b.wait("act", "pe", Wc[u - 4])
                                b.wait("act", "dve", Rc[u - 4])
                            Vc[u] = b.cp("act", qraw[:, p4 * 512: p4 * 512 + w],
                                         pq[p3][:, 0:w])
                        u = i
                        if 0 <= u < nu:
                            d, s0, w = units[u]
                            p3 = u % 3
                            if u >= 3:
                                b.wait("pe", "act", Vc[u - 3])
                            for k in range(KC):
                                Mc[u] = b.mm(
                                    pq[p3][:, 0:w],
                                    wtens[:, k * 1024 + d * 128:
                                          k * 1024 + (d + 1) * 128],
                                    xc2[:, k * R + off + s0:
                                        k * R + off + s0 + w],
                                    k == 0, k == KC - 1)
                        u = i - 3
                        if 0 <= u < nu:
                            d, s0, w = units[u]
                            p3 = u % 3
                            p4 = u % 4
                            b.wait("pe", "act", Vc[u])
                            if u >= 3:
                                b.wait("pe", "dve", Rc[u - 3])
                            Wc[u] = b.mm(pw[p3][:, 0:w], pswap_sb[:],
                                         qraw[:, p4 * 512: p4 * 512 + w],
                                         True, True)
                    b.wait("pe", "dve", b.count["dve"])
                    b.wait("pe", "act", b.count["act"])

            prev_outq = 0
            wout_pe = 0
            b.wait("pool", "dve", b.count["dve"])
            b.dma("pool", "dpl", crep_sb[:], crep_d[:, 0:R])
            b.dma("pool", "dpl", srep_sb[:], srep_d[:, 0:R])
            cs_pl = b.count["dpl"]
            lv = wload("A", 2, wqkv_d[:, 2048:3072])
            lq = wload("B", 0, wqkv_d[:, 0:1024])
            for Q in range(cfg.nq):
                R0 = Q * cfg.qown
                b.wait("dve", "dpl", cs_pl)
                emit_stats(f"a{Q}", R0, R, meanr[:], rstdr[:], None)
                with ExitStack() as pscope:
                    pb = pscope.enter_context(
                        nc.psum_tensor(f"pba{Q}", [128, 512], F32))
                    emit_cols(rstdr[:], nvt, rcolq, pb)
                    for (t0, w) in cfg.rsubs():
                        b.wait("pe", "dve", b.count["dve"])
                        b.wait("pe", "act", b.count["act"])
                        mmc = b.mm(pb[:, 0:w], ones1[:],
                                   rstdr[0:1, t0:t0 + w], True, True)
                        b.wait("dve", "pe", mmc)
                        c = b.tt("dve", crep_sb[:, t0:t0 + w],
                                 crep_sb[:, t0:t0 + w], pb[:, 0:w], ALU.mult)
                        b.wait("dve", "dve", c)
                        c = b.tt("dve", srep_sb[:, t0:t0 + w],
                                 srep_sb[:, t0:t0 + w], pb[:, 0:w], ALU.mult)
                        b.wait("pe", "dve", c)
                    for (t0, w) in cfg.rsubs():
                        b.wait("pe", "dve", b.count["dve"])
                        mmc = b.mm(pb[:, 0:w], ones1[:], meanr[0:1, t0:t0 + w],
                                   True, True)
                        b.wait("dve", "pe", mmc)
                        ev = 0
                        for k in range(KC):
                            ev = b.tt("dve",
                                      xc2[:, k * R + t0: k * R + t0 + w],
                                      x_sb[:, k * T + R0 + t0:
                                           k * T + R0 + t0 + w],
                                      pb[:, 0:w], ALU.subtract)
                        b.wait("pe", "dve", ev)
                if Q > 0:
                    b.wait("dve", "dve", b.count["dve"])
                    p0 = Q * cfg.qown
                    for d in range(KC):
                        b.tt("dve", x_sb[:, d * T + p0: d * T + p0 + 128],
                             x_sb[:, d * T + p0: d * T + p0 + 128],
                             outq[:, d * 128:(d + 1) * 128], ALU.add)

                with ExitStack() as pscope:
                    pv = pscope.enter_context(
                        nc.psum_tensor(f"pv{Q}", [128, 1024], F32))
                    ev_sl = [0, 0]
                    b.wait("pe", "dve", b.count["dve"])
                    b.wait("dve", "dve", b.count["dve"])
                    wwait("A", lv)
                    for tt in range(nvt):
                        for vc in range(2):
                            sl = (tt * 2 + vc) % 2
                            if ev_sl[sl]:
                                b.wait("pe", "dve", ev_sl[sl])
                            for k in range(KC):
                                mmc = b.mm(
                                    pv[:, sl * 512: sl * 512 + 512],
                                    xc2[:, k * R + tt * 128:
                                        k * R + (tt + 1) * 128],
                                    wbufA[:, k * 1024 + vc * 512:
                                          k * 1024 + (vc + 1) * 512],
                                    k == 0, k == KC - 1)
                            b.wait("dve", "pe", mmc)
                            ev_sl[sl] = b.ts(
                                "dve",
                                v_sb[:, tt * 1024 + vc * 512:
                                     tt * 1024 + (vc + 1) * 512],
                                pv[:, sl * 512: sl * 512 + 512],
                                rcolq[:, tt: tt + 1], ALU.mult)
                    b.wait("pe", "dve", max(ev_sl))
                    wdone("A")
                lk = wload("A", 1, wqkv_d[:, 1024:2048])

                wwait("B", lq)
                qsubs = [(s, min(512, cfg.qown - s))
                         for s in range(0, cfg.qown, 512)]
                produce_qk(Q, R0, q_sb, cfg.qown, "B", qsubs, 128)
                wdone("B")
                lo = wload("B", 3, wout_d[:, :])
                wwait("A", lk)
                produce_qk(Q, R0, k_sb, R, "A", cfg.rsubs(), 0)
                wdone("A")
                if Q + 1 < cfg.nq:
                    b.wait("pool", "dve", b.count["dve"])
                    R0n = (Q + 1) * cfg.qown
                    b.dma("pool", "dpl", crep_sb[:], crep_d[:, R0n:R0n + R])
                    b.dma("pool", "dpl", srep_sb[:], srep_d[:, R0n:R0n + R])
                    cs_pl = b.count["dpl"]
                    lv = wload("A", 2, wqkv_d[:, 2048:3072])

                wwait("B", lo)
                with ExitStack() as pscope:
                    ps_s = pscope.enter_context(
                        nc.psum_tensor(f"pss{Q}", [128, 2048], F32))
                    ps_pt = [pscope.enter_context(
                        nc.psum_tensor(f"pspt{pp}{Q}", [128, 768], BF16))
                        for pp in range(2)]
                    ps_o = [pscope.enter_context(
                        nc.psum_tensor(f"pso{pp}{Q}", [128, 128], F32))
                        for pp in range(2)]
                    n = cfg.qpq * 8
                    slots = [(qi, hp) for qi in range(cfg.qpq)
                             for hp in range(8)]
                    Acnt = [0] * n
                    Bcnt = [0] * n
                    Ccnt = [0] * n
                    Dcnt = [0] * n
                    Ecnt = [0] * n
                    Fcnt = [0] * n
                    Gcnt = [0] * n
                    Hcnt = [0] * n
                    Xd = [0] * n

                    def bsel_of(j):
                        gqt = Q * cfg.qpq + slots[j][0]
                        return 1 if gqt == 0 else (
                            2 if gqt == cfg.last_qt else 0)

                    b.wait("pe", "dve", b.count["dve"])
                    for i in range(n + 8):
                        j = i - 2
                        if 0 <= j < n:
                            pr = j % 2
                            b.wait("act", "pe", Acnt[j])
                            if j >= 2:
                                b.wait("act", "dve", Dcnt[j - 2])
                            for jh in range(2):
                                Ccnt[j] = b.actv(
                                    p32[:, pr * 768 + jh * 384:
                                        pr * 768 + (jh + 1) * 384],
                                    ps_s[:, (2 * pr + jh) * 512:
                                         (2 * pr + jh) * 512 + 384],
                                    AF.Exp, bias=negc[:, 0:1],
                                    scale=0.125,
                                    accum=lr2[:, 2 * pr + jh:2 * pr + jh + 1])
                        if i < n:
                            qi, hp = slots[i]
                            bsel = bsel_of(i)
                            if i >= 2:
                                b.wait("pe", "act", Ccnt[i - 2])
                            for jh in range(2):
                                hb = jh * 64
                                sb2 = ((i % 2) * 2 + jh) * 512
                                b.mm(ps_s[:, sb2: sb2 + 384], idb[:],
                                     biasq_sb[:, bsel * 384:(bsel + 1) * 384],
                                     True, False)
                                Acnt[i] = b.mm(
                                    ps_s[:, sb2: sb2 + 384],
                                    q_sb[hb:hb + 64, hp * cfg.qown + qi * 128:
                                         hp * cfg.qown + (qi + 1) * 128],
                                    k_sb[hb:hb + 64, hp * R + qi * 128:
                                         hp * R + qi * 128 + 384],
                                    False, True)
                        j = i - 3
                        if 0 <= j < n:
                            pr = j % 2
                            b.wait("dve", "act", Ccnt[j])
                            c = b.recip(rc2[:, 2 * pr:2 * pr + 2],
                                        lr2[:, 2 * pr:2 * pr + 2])
                            b.wait("dve", "dve", c)
                            if j >= 2:
                                b.wait("dve", "pe", Ecnt[j - 2])
                            for jh in range(2):
                                Dcnt[j] = b.ts(
                                    "dve",
                                    pn_bf[:, pr * 768 + jh * 384:
                                          pr * 768 + (jh + 1) * 384],
                                    p32[:, pr * 768 + jh * 384:
                                        pr * 768 + (jh + 1) * 384],
                                    rc2[:, 2 * pr + jh:2 * pr + jh + 1],
                                    ALU.mult)
                        j = i - 4
                        if 0 <= j < n:
                            pr = j % 2
                            b.wait("pe", "dve", Dcnt[j])
                            if j >= 2:
                                b.wait("pe", "act", Fcnt[j - 2])
                            for kb in range(6):
                                Ecnt[j] = b.tr(
                                    ps_pt[pr][:, kb * 128:(kb + 1) * 128],
                                    pn_bf[:, pr * 768 + kb * 128:
                                          pr * 768 + (kb + 1) * 128],
                                    idb[:])
                        j = i - 5
                        if 0 <= j < n:
                            pr = j % 2
                            b.wait("act", "pe", Ecnt[j])
                            if j >= 2:
                                b.wait("act", "pe", Gcnt[j - 2])
                            Fcnt[j] = b.cp("act",
                                           pt_sb[:, pr * 768:(pr + 1) * 768],
                                           ps_pt[pr][:])
                        j = i - 6
                        if 0 <= j < n:
                            pr = j % 2
                            qi1, hp1 = slots[j]
                            b.wait("pe", "act", Fcnt[j])
                            if j >= 2:
                                b.wait("pe", "act", Hcnt[j - 2])
                            for jh in range(2):
                                h1 = hp1 * 2 + jh
                                hb1 = jh * 64
                                for kb in range(3):
                                    Gcnt[j] = b.mm(
                                        ps_o[pr][hb1:hb1 + 64, :],
                                        v_sb[:, (qi1 + kb) * 1024 + h1 * 64:
                                             (qi1 + kb) * 1024 +
                                             (h1 + 1) * 64],
                                        pt_sb[:, pr * 768 + jh * 384 +
                                              kb * 128:
                                              pr * 768 + jh * 384 +
                                              (kb + 1) * 128],
                                        kb == 0, kb == 2,
                                        tile_position=(0, hb1))
                        j = i - 7
                        if 0 <= j < n:
                            pr = j % 2
                            qi1, hp1 = slots[j]
                            b.wait("act", "pe", Gcnt[j])
                            if wout_pe:
                                b.wait("act", "pe", wout_pe)
                            Hcnt[j] = b.cp(
                                "act",
                                attnq4[:, hp1 * qtw + qi1 * 128:
                                       hp1 * qtw + (qi1 + 1) * 128],
                                ps_o[pr][:])
                    b.wait("pe", "act", b.count["act"])
                    wev = {0: 0, 1: 0}
                    for dout in range(KC):
                        wp = dout % 2
                        if wev[wp]:
                            b.wait("pe", "dve", wev[wp])
                        for k in range(KC):
                            mmc = b.mm(ps_s[:, wp * 1024: wp * 1024 + qtw],
                                       wbufB[:, k * 1024 + dout * 128:
                                             k * 1024 + (dout + 1) * 128],
                                       attnq4[:, k * qtw:(k + 1) * qtw],
                                       k == 0, k == KC - 1)
                        b.wait("dve", "pe", mmc)
                        b.wait("dve", "dve", b.count["dve"])
                        abs0 = 128 + Q * cfg.qown
                        nlast = qtw - 128
                        if nlast > 0:
                            b.stt("dve",
                                  x_sb[:, dout * T + abs0:
                                       dout * T + abs0 + nlast],
                                  ps_s[:, wp * 1024: wp * 1024 + nlast],
                                  bout_sb[:, dout:dout + 1],
                                  x_sb[:, dout * T + abs0:
                                       dout * T + abs0 + nlast],
                                  ALU.add, ALU.add)
                        wev[wp] = b.ts(
                            "dve",
                            outq[:, dout * 128:(dout + 1) * 128],
                            ps_s[:, wp * 1024 + nlast:
                                 wp * 1024 + nlast + 128],
                            bout_sb[:, dout:dout + 1], ALU.add)
                    wout_pe = b.count["pe"]
                    wdone("B")
                if Q + 1 < cfg.nq:
                    lq = wload("B", 0, wqkv_d[:, 0:1024])
                prev_outq = b.count["dve"]
            b.wait("dve", "dve", b.count["dve"])
            p0 = cfg.nq * cfg.qown
            for d in range(KC):
                b.tt("dve", x_sb[:, d * T + p0: d * T + p0 + 128],
                     x_sb[:, d * T + p0: d * T + p0 + 128],
                     outq[:, d * 128:(d + 1) * 128], ALU.add)

    def barrier():
        for e in ENGINES:
            for s in ("pe", "dve", "act", "pool", "dsp", "dpl",
                      "w0", "w1", "w2", "w3",
                      "tee0", "tee1", "tee2", "tee3"):
                b.wait(e, s)

    if "ffn0" in phases:
        emit_ffn(0)
        barrier()
    if "attn" in phases:
        emit_attn()
        barrier()
    if "ffn1" in phases:
        emit_ffn(1)
        barrier()

    b.wait("sp", "dve", b.count["dve"])
    b.wait("sp", "pool", b.count["pool"])
    for d in range(KC):
        b.dma("sp", "dso", out_d[d * 128:(d + 1) * 128, :],
              x_sb[:, d * T + 128: d * T + 128 + cfg.own])
    b.wait("sp", "dso", b.count["dso"])

    with nc.Block() as block:
        @block.tensor
        def _(t):
            for fn in b.ops["pe"]:
                fn(t)

        @block.vector
        def _(v):
            for fn in b.ops["dve"]:
                fn(v)

        @block.scalar
        def _(s):
            for fn in b.ops["act"]:
                fn(s)

        @block.gpsimd
        def _(g):
            for fn in b.ops["pool"]:
                fn(g)

        @block.sync
        def _(s):
            for fn in b.ops["sp"]:
                fn(s)

    stack.close()
    return nc


def make_consts():
    ident = np.eye(128, dtype=np.float32)
    pswap = np.zeros((128, 128), dtype=np.float32)
    for j in range(128):
        i = (j + 32) % 64 + 64 * (j // 64)
        pswap[i, j] = 1.0
    return ident, pswap


def make_band_bias(left_edge, right_edge):
    i = np.arange(128)[:, None]
    j = np.arange(384)[None, :]
    band = np.where((j >= i) & (j <= i + 256), 0.0, NEG).astype(np.float32)
    first = band.copy()
    if left_edge:
        first[:, 0:128] = NEG
    last = band.copy()
    if right_edge:
        last[:, 256:384] = NEG
    out = np.ascontiguousarray(np.concatenate([band, first, last], axis=1))
    for v in range(3):
        assert np.all(out[:, v * 384 + 128: v * 384 + 256] == 0.0)
    return out


def make_rope(positions):
    inv = 1.0 / (10000.0 ** (np.arange(0, ROT * 2, 2, dtype=np.float32)
                             / (ROT * 2)))
    fr = positions[None, :].astype(np.float32) * inv[:, None]
    c = np.cos(fr).astype(np.float32)
    s = np.sin(fr).astype(np.float32)
    crep = np.ascontiguousarray(np.concatenate([c, c, c, c], axis=0))
    srep = np.ascontiguousarray(np.concatenate([-s, s, -s, s], axis=0))
    return crep, srep


def core_inputs(cfg: CFG, x_seq, seq_start, seq_len):
    own, halo, T = cfg.own, cfg.halo, cfg.T
    xe = np.zeros((T, D), dtype=np.float32)
    xe[halo:halo + own] = x_seq[seq_start:seq_start + own]
    left_edge = seq_start < halo
    if not left_edge:
        xe[0:halo] = x_seq[seq_start - halo:seq_start]
    right_edge = seq_start + own + halo > seq_len
    if not right_edge:
        xe[halo + own:] = x_seq[seq_start + own:seq_start + own + halo]
    xT = np.ascontiguousarray(xe.T)
    pos = np.arange(seq_start - halo, seq_start + own + halo)
    crep, srep = make_rope(pos)
    biasq = make_band_bias(left_edge, right_edge)
    return xT, crep, srep, biasq


def kernel(L0_attn_x, L0_attn_Wqkv, L0_attn_Wout, L0_attn_bout,
           L0_ffn_gate_w, L0_ffn_expert_w):
    cfg = CFG(own=2048, qown=512)
    x = np.asarray(L0_attn_x, dtype=np.float32)
    wqkv = np.ascontiguousarray(np.asarray(L0_attn_Wqkv, dtype=np.float32))
    wout = np.ascontiguousarray(np.asarray(L0_attn_Wout, dtype=np.float32))
    bout = np.asarray(L0_attn_bout, dtype=np.float32)
    gw = np.asarray(L0_ffn_gate_w, dtype=np.float32)
    wexp = np.asarray(L0_ffn_expert_w, dtype=np.float32)

    boutT = np.ascontiguousarray(bout.reshape(8, 128).T)
    gwT = np.ascontiguousarray(gw.T)
    wexpT = np.ascontiguousarray(wexp.transpose(0, 2, 1))
    ident, pswap = make_consts()

    B_, S, _ = x.shape
    n_cores = 8
    per_batch = n_cores // B_
    in_maps = []
    for c in range(n_cores):
        bidx, qidx = c // per_batch, c % per_batch
        xT, crep, srep, biasq = core_inputs(cfg, x[bidx], qidx * cfg.own, S)
        in_maps.append({
            "xT": xT, "wqkv": wqkv, "wout": wout, "boutT": boutT,
            "gwT": gwT, "wexpT": wexpT, "ident": ident,
            "pswap": pswap, "crep": crep, "srep": srep, "biasq": biasq,
        })

    nc = build_program(cfg)
    res = run_bass_kernel_spmd(nc, in_maps, core_ids=list(range(n_cores)))
    out = np.empty_like(x)
    for c in range(n_cores):
        bidx, qidx = c // per_batch, c % per_batch
        out[bidx, qidx * cfg.own:(qidx + 1) * cfg.own, :] = \
            res.results[c]["outT"].T
    return out


# version 49
# speedup vs baseline: 1.0237x; 1.0208x over previous
from contextlib import ExitStack

import numpy as np

import concourse.bass as bass
import concourse.mybir as mybir
from concourse.bass_utils import run_bass_kernel_spmd

F32 = mybir.dt.float32
BF16 = mybir.dt.bfloat16
AF = mybir.ActivationFunctionType
ALU = mybir.AluOpType
AX = mybir.AxisListType
NEG = -1.0e30

D = 1024
KC = 8
ROT = 32
EPS = 1e-5


class CFG:
    def __init__(self, own=2048, qown=512):
        self.own = own
        self.halo = 128
        self.T = own + 2 * self.halo
        self.qown = qown
        self.nq = own // qown
        self.R = qown + 256
        self.qpq = qown // 128
        self.ttiles = self.T // 128
        self.last_qt = own // 128 - 1

    def tchunks(self):
        out, t = [], 0
        while t < self.T:
            w = min(512, self.T - t)
            out.append((t, w))
            t += w
        return out

    def rsubs(self):
        out, t = [], 0
        while t < self.R:
            w = min(512, self.R - t)
            out.append((t, w))
            t += w
        return out


ENGINES = ("pe", "dve", "act", "pool", "sp")


class B:

    def __init__(self, nc, stack):
        self.nc = nc
        self.ops = {e: [] for e in ENGINES}
        self.sems = {}
        self.count = {}
        self.hw = {}
        self.stack = stack

    def sem(self, name):
        self.sems[name] = self.stack.enter_context(self.nc.semaphore(name))
        self.count[name] = 0

    def emit(self, eng, fn, inc=None, n=1):
        if inc is None:
            self.ops[eng].append(fn)
            return 0
        self.count[inc] += n
        s = self.sems[inc]
        self.ops[eng].append(lambda E, fn=fn, s=s, n=n: fn(E).then_inc(s, n))
        return self.count[inc]

    def wait(self, eng, sem, val=None):
        if val is None:
            val = self.count[sem]
        if val <= 0:
            return
        key = (eng, sem)
        if self.hw.get(key, 0) >= val:
            return
        self.hw[key] = val
        s = self.sems[sem]
        self.ops[eng].append(lambda E, s=s, v=val: E.wait_ge(s, v))

    def mm(self, ps, lt, rs, st, sp, tile_position=None):
        return self.emit(
            "pe",
            lambda E, a=ps, b=lt, c=rs, st=st, sp=sp, tp=tile_position:
                E.matmul(a, b, c, start=st, stop=sp, tile_position=tp),
            inc="pe")

    def tr(self, ps, in_, ident):
        return self.emit(
            "pe", lambda E, a=ps, b=in_, c=ident: E.transpose(a, b, c),
            inc="pe")

    def tt(self, eng, out, a0, a1, op):
        inc = "dve" if eng == "dve" else "pool"
        return self.emit(
            eng, lambda E, o=out, x=a0, y=a1, op=op: E.tensor_tensor(o, x, y, op),
            inc=inc)

    def ts(self, eng, out, a0, s1, op):
        inc = "dve" if eng == "dve" else "pool"
        return self.emit(
            eng,
            lambda E, o=out, x=a0, s=s1, op=op: E.tensor_scalar(o, x, s, None, op),
            inc=inc)

    def stt(self, eng, out, a0, s, a1, op0, op1):
        inc = {"dve": "dve", "act": "act"}.get(eng, "pool")
        return self.emit(
            eng,
            lambda E, o=out, x=a0, s=s, y=a1, p=op0, q=op1:
                E.scalar_tensor_tensor(o, x, s, y, p, q),
            inc=inc)

    def cp(self, eng, out, in_):
        if eng == "act":
            return self.emit(
                "act", lambda E, o=out, i=in_: E.copy(o, i), inc="act")
        inc = "dve" if eng == "dve" else "pool"
        return self.emit(
            eng, lambda E, o=out, i=in_: E.tensor_copy(o, i), inc=inc)

    def red(self, out, in_, op=ALU.max, negate=False):
        return self.emit(
            "dve",
            lambda E, o=out, i=in_, op=op, ng=negate:
                E.tensor_reduce(o, i, AX.X, op, negate=ng),
            inc="dve")

    def actv(self, out, in_, func, bias=0.0, scale=1.0, accum=None):
        return self.emit(
            "act",
            lambda E, o=out, i=in_, f=func, b=bias, s=scale, a=accum:
                E.activation(o, i, f, bias=b, scale=s, accum_out=a),
            inc="act")

    def recip(self, out, in_):
        return self.emit(
            "dve", lambda E, o=out, i=in_: E.reciprocal(o, i), inc="dve")

    def dma(self, eng, sem, out, in_):
        return self.emit(
            eng, lambda E, o=out, i=in_: E.dma_start(o, i), inc=sem, n=16)

    def memset(self, eng, ap, val):
        inc = "dve" if eng == "dve" else "pool"
        return self.emit(
            eng, lambda E, a=ap, v=val: E.memset(a, v), inc=inc)


def build_program(cfg: CFG, phases=("ffn0", "attn", "ffn1")):
    nc = bass.Bass("TRN2", target_bir_lowering=False)
    T, R = cfg.T, cfg.R

    xT_d = nc.dram_tensor("xT", [D, T], F32, kind="ExternalInput").ap()
    wqkv_d = nc.dram_tensor("wqkv", [D, 3 * D], F32, kind="ExternalInput").ap()
    wout_d = nc.dram_tensor("wout", [D, D], F32, kind="ExternalInput").ap()
    bout_d = nc.dram_tensor("boutT", [128, 8], F32, kind="ExternalInput").ap()
    gwT_d = nc.dram_tensor("gwT", [D, 8], F32, kind="ExternalInput").ap()
    wexp_d = nc.dram_tensor("wexpT", [8, D, D], F32, kind="ExternalInput").ap()
    ident_d = nc.dram_tensor("ident", [128, 128], F32, kind="ExternalInput").ap()
    pswap_d = nc.dram_tensor("pswap", [128, 128], F32, kind="ExternalInput").ap()
    crep_d = nc.dram_tensor("crep", [128, T], F32, kind="ExternalInput").ap()
    srep_d = nc.dram_tensor("srep", [128, T], F32, kind="ExternalInput").ap()
    biasq_d = nc.dram_tensor("biasq", [128, 3 * 384], F32, kind="ExternalInput").ap()
    out_d = nc.dram_tensor("outT", [D, cfg.own], F32, kind="ExternalOutput").ap()
    webf_d = nc.dram_tensor("wexp_bf", [8, 128, KC * D], BF16,
                            kind="Internal").ap()
    wqbf_d = nc.dram_tensor("wqkv_bf", [4, 128, KC * D], BF16,
                            kind="Internal").ap()
    oht_d = nc.dram_tensor("onehot_sc", [8, T], BF16, kind="Internal").ap()

    stack = ExitStack()
    b = B(nc, stack)
    for s in ("pe", "dve", "act", "pool", "dsp", "dpl", "w0", "w1", "w2",
              "w3", "tee0", "tee1", "tee2", "tee3", "mk0", "mk1", "dso",
              "dx0", "dx1", "dx2", "dx3", "dx4", "dx5", "dx6", "dx7",
              "w4", "tee4"):
        b.sem(s)

    def sbuf(name, shape, dt):
        return stack.enter_context(nc.sbuf_tensor(name, shape, dt))

    x_sb = sbuf("x_sb", [128, KC * T], F32)
    bout_sb = sbuf("bout_sb", [128, 8], F32)
    gwT_sb = sbuf("gwT_sb", [128, KC * 8], F32)
    idb = sbuf("idb", [128, 128], BF16)
    pswap_sb = sbuf("pswap_sb", [128, 128], BF16)
    biasq_sb = sbuf("biasq_sb", [128, 3 * 384], BF16)
    ones_bf = sbuf("ones_bf", [128, 1], BF16)
    onescol = sbuf("onescol", [128, 1], F32)
    ones1 = sbuf("ones1", [1, 128], F32)
    xb_g = sbuf("xb_g", [128, 2048], BF16)
    xq_g = sbuf("xq_g", [128, 2048], BF16)
    rowtmp = sbuf("rowtmp", [1, T], F32)

    for k in range(KC):
        b.dma("sp", f"dx{k}", x_sb[:, k * T:(k + 1) * T],
              xT_d[k * 128:(k + 1) * 128, :])
    b.dma("sp", "dsp", bout_sb[:], bout_d[:])
    for k in range(KC):
        b.dma("sp", "dsp", gwT_sb[:, k * 8:(k + 1) * 8], gwT_d[k * 128:(k + 1) * 128, :])
    init_sp = b.count["dsp"]
    b.dma("pool", "dpl", idb[:], ident_d[:])
    b.dma("pool", "dpl", pswap_sb[:], pswap_d[:])
    b.dma("pool", "dpl", biasq_sb[:], biasq_d[:])
    init_pl = b.count["dpl"]
    b.memset("dve", ones_bf[:], 1.0)
    b.memset("dve", onescol[:], 1.0)
    b.memset("dve", ones1[:], 1.0)
    init_dve = b.count["dve"]

    def emit_stats(tag, a, L, meanr, rstdr, rmr):
        b.wait("dve", "dve", b.count["dve"])
        b.wait("act", "dve", b.count["dve"])
        with ExitStack() as pscope:
            p1 = pscope.enter_context(nc.psum_tensor(f"p1{tag}", [1, 512], F32))
            p2 = pscope.enter_context(nc.psum_tensor(f"p2{tag}", [1, 512], F32))
            t = 0
            pe_slot = {i: 0 for i in range(4)}
            while t < L:
                w = min(512, L - t)
                for k in range(KC):
                    sl = k % 4
                    if pe_slot[sl]:
                        b.wait("dve", "pe", pe_slot[sl])
                        b.wait("act", "pe", pe_slot[sl])
                    b.wait("dve", f"dx{k}", 16)
                    b.wait("act", f"dx{k}", 16)
                    cast_c = b.cp("dve", xb_g[:, sl * 512: sl * 512 + w],
                                  x_sb[:, k * T + a + t: k * T + a + t + w])
                    sq_c = b.actv(xq_g[:, sl * 512: sl * 512 + w],
                                  x_sb[:, k * T + a + t: k * T + a + t + w],
                                  AF.Square)
                    b.wait("pe", "dve", cast_c)
                    b.wait("pe", "act", sq_c)
                    b.mm(p1[:, 0:w], ones_bf[:], xb_g[:, sl * 512: sl * 512 + w],
                         k == 0, k == KC - 1)
                    pe_slot[sl] = b.mm(
                        p2[:, 0:w], ones_bf[:], xq_g[:, sl * 512: sl * 512 + w],
                        k == 0, k == KC - 1)
                b.wait("dve", "pe", b.count["pe"])
                b.cp("dve", meanr[0:1, t:t + w], p1[:, 0:w])
                ev = b.cp("dve", rstdr[0:1, t:t + w], p2[:, 0:w])
                b.wait("pe", "dve", ev)
                t += w
        b.wait("act", "dve", b.count["dve"])
        c = b.actv(meanr, meanr, AF.Copy, scale=1.0 / D)
        c = b.actv(rstdr, rstdr, AF.Copy, scale=1.0 / D)
        tmp = rowtmp[0:1, 0:L]
        b.wait("act", "act", c)
        c = b.actv(tmp, meanr, AF.Square)
        b.wait("dve", "act", c)
        c = b.tt("dve", rstdr, rstdr, tmp, ALU.subtract)
        b.wait("dve", "dve", c)
        c = b.ts("dve", rstdr, rstdr, EPS, ALU.add)
        b.wait("act", "dve", c)
        c = b.actv(tmp, rstdr, AF.Ln)
        b.wait("act", "act", c)
        c = b.actv(rstdr, tmp, AF.Exp, scale=-0.5)
        if rmr is not None:
            b.wait("dve", "act", c)
            b.tt("dve", rmr, meanr, rstdr, ALU.mult)

    def emit_bcast(row, L, dst, pbank):
        t = 0
        while t < L:
            w = min(512, L - t)
            b.wait("pe", "dve", b.count["dve"])
            b.wait("pe", "act", b.count["act"])
            mmc = b.mm(pbank[:, 0:w], ones1[:], row[0:1, t:t + w], True, True)
            b.wait("dve", "pe", mmc)
            ev = b.cp("dve", dst[:, t:t + w], pbank[:, 0:w])
            b.wait("pe", "dve", ev)
            t += w

    def emit_cols(row, ntiles, dst, pbank):
        b.wait("pe", "dve", b.count["dve"])
        b.wait("pe", "act", b.count["act"])
        for tt in range(ntiles):
            b.mm(pbank[:, tt: tt + 1],
                 row[0:1, tt * 128:(tt + 1) * 128], ones1[0:1, 0:1],
                 True, True)
        b.wait("dve", "pe", b.count["pe"])
        ev = b.cp("dve", dst[:, 0:ntiles], pbank[:, 0:ntiles])
        b.wait("pe", "dve", ev)

    def emit_ffn(ffn_idx):
        tag = f"f{ffn_idx}"
        with ExitStack() as fs:
            sb = lambda n, sh, dt: fs.enter_context(
                nc.sbuf_tensor(n + tag, sh, dt))
            meanr = sb("meanr", [1, T], F32)
            rstdr = sb("rstdr", [1, T], F32)
            rmr = rowtmp[0:1, 0:T]
            emit_stats(tag, 0, T, meanr[:], rstdr[:], rmr)

            hrb_sb = sb("hrb", [128, T], F32)
            mb_sb = sb("mb", [128, T], F32)
            rcol = sb("rcol", [128, cfg.ttiles], F32)
            rmcol = sb("rmcol", [128, cfg.ttiles], F32)

            onehotT = sb("onehotT", [8, T], BF16)
            g1T = sb("g1T", [128, 8], F32)
            g1r = sb("g1r", [1, 8], F32)
            nt = cfg.ttiles
            ohbuf = sb("ohbuf", [128, 8 * nt], BF16)
            u144 = sb("u144", [128, 8 * nt], F32)
            v144 = sb("v144", [128, 8 * nt], F32)
            c144 = sb("c144", [128, 8 * nt], F32)
            m18 = sb("m18", [128, nt], F32)
            with ExitStack() as pscope:
                pg1 = pscope.enter_context(
                    nc.psum_tensor("pg1" + tag, [1, 8], F32))
                pgb = pscope.enter_context(
                    nc.psum_tensor("pgb" + tag, [128, 8], F32))
                pg = pscope.enter_context(
                    nc.psum_tensor("pg" + tag, [128, 8 * nt], F32))
                pot = [pscope.enter_context(
                    nc.psum_tensor(f"pot{pp}" + tag, [8, 128], BF16))
                    for pp in range(2)]
                b.wait("pe", "dsp", init_sp)
                b.wait("pe", "dve", init_dve)
                for k in range(KC):
                    b.mm(pg1[:], onescol[:], gwT_sb[:, k * 8:(k + 1) * 8],
                         k == 0, k == KC - 1)
                b.wait("dve", "pe", b.count["pe"])
                ev = b.cp("dve", g1r[:], pg1[:])
                b.wait("pe", "dve", ev)
                mmc = b.mm(pgb[:], ones1[:], g1r[:], True, True)
                b.wait("dve", "pe", mmc)
                b.cp("dve", g1T[:], pgb[:])
                for tt in range(nt):
                    for k in range(KC):
                        b.wait("pe", f"dx{k}", 16)
                        b.mm(pg[:, tt * 8: tt * 8 + 8],
                             x_sb[:, k * T + tt * 128: k * T + (tt + 1) * 128],
                             gwT_sb[:, k * 8:(k + 1) * 8],
                             k == 0, k == KC - 1)
                gate_pe = b.count["pe"]
                with ExitStack() as bscope:
                    pb = bscope.enter_context(
                        nc.psum_tensor("pb" + tag, [128, 512], F32))
                    emit_cols(rstdr[:], cfg.ttiles, rcol, pb)
                    emit_cols(rmr, cfg.ttiles, rmcol, pb)
                    emit_bcast(meanr[:], T, mb_sb, pb)
                    b.wait("dve", "act", b.count["act"])
                    c = b.ts("dve", rstdr[:], rstdr[:], 0.5, ALU.mult)
                    b.wait("dve", "dve", c)
                    emit_bcast(rstdr[:], T, hrb_sb, pb)
                b.wait("dve", "pe", gate_pe)
                b.wait("dve", "dve", b.count["dve"])
                pg3 = pg[:].rearrange("p (t e) -> p t e", e=8)
                u3 = u144[:].rearrange("p (t e) -> p t e", e=8)
                v3 = v144[:].rearrange("p (t e) -> p t e", e=8)
                c3 = c144[:].rearrange("p (t e) -> p t e", e=8)
                oh3 = ohbuf[:].rearrange("p (t e) -> p t e", e=8)
                rcol_b = rcol[:, 0:nt].unsqueeze(2).broadcast_to((128, nt, 8))
                rmcol_b = rmcol[:, 0:nt].unsqueeze(2).broadcast_to((128, nt, 8))
                g1_b = g1T[:].unsqueeze(1).broadcast_to((128, nt, 8))
                m_b = m18[:, 0:nt].unsqueeze(2).broadcast_to((128, nt, 8))
                c = b.tt("dve", u3, pg3, rcol_b, ALU.mult)
                b.wait("dve", "dve", c)
                c = b.tt("dve", v3, g1_b, rmcol_b, ALU.mult)
                b.wait("dve", "dve", c)
                c = b.tt("dve", u3, u3, v3, ALU.subtract)
                b.wait("dve", "dve", c)
                c = b.red(m18[:, 0:nt].unsqueeze(2), u3)
                b.wait("dve", "dve", c)
                c = b.tt("dve", v3, u3, m_b, ALU.is_equal)
                b.wait("dve", "dve", c)
                c = b.cp("dve", c3[:, :, 0:1], v3[:, :, 0:1])
                for e in range(1, 8):
                    b.wait("dve", "dve", c)
                    c = b.tt("dve", c3[:, :, e:e + 1], c3[:, :, e - 1:e],
                             v3[:, :, e:e + 1], ALU.add)
                b.wait("dve", "dve", c)
                c = b.ts("dve", u3, c3, 1.0, ALU.is_equal)
                b.wait("dve", "dve", c)
                c = b.tt("dve", oh3, v3, u3, ALU.mult)
                b.wait("dve", "dve", c)
                b.wait("pe", "dve", b.count["dve"])
                b.wait("pe", "dpl", init_pl)
                last_ev = {0: 0, 1: 0}
                for tt in range(nt):
                    sl = tt % 2
                    if last_ev[sl]:
                        b.wait("pe", "dve", last_ev[sl])
                    trc = b.tr(pot[sl][:],
                               ohbuf[:, tt * 8: tt * 8 + 8], idb[:])
                    b.wait("dve", "pe", trc)
                    last_ev[sl] = b.cp(
                        "dve", onehotT[:, tt * 128:(tt + 1) * 128],
                        pot[sl][:])

            b.wait("sp", "dve", b.count["dve"])
            oh_w = b.dma("sp", "dsp", oht_d[:, :], onehotT[:, :])

            xc = sb("xc", [128, 2 * KC * 512], BF16)
            mkbuf = sb("mkbuf", [128, 2 * 8 * 512], BF16)
            webuf = sb("webuf", [128, 5 * 2048], BF16)
            lmbuf = sb("lmbuf", [128, 4 * 1024], BF16)
            ev8 = sb("ev8", [128, KC * 512], BF16)
            uscr = sb("uscr", [128, 2 * 512], F32)
            pd = fs.enter_context(nc.psum_tensor("pd" + tag, [128, 4096], F32))
            wsem = ["w0", "w1", "w2", "w3", "w4"]
            tsem = ["tee0", "tee1", "tee2", "tee3", "tee4"]
            wload = [b.count[s] // 16 for s in wsem]
            wfree = [0, 0, 0, 0, 0]
            wtee = [0, 0, 0, 0, 0]
            lmfree = [0, 0, 0, 0]
            sidx = 0
            piece_tee = [0] * 32
            piece_tee_sem = [""] * 32
            tcs = cfg.tchunks()

            x3 = x_sb[:].rearrange("p (k t) -> p k t", t=T)
            pd3 = pd[:].rearrange("p (d f) -> p d f", f=512)
            ev3 = ev8[:].rearrange("p (d f) -> p d f", f=512)
            mkload = [0, 0]

            def xc3(par):
                return xc[:, par * KC * 512:(par + 1) * KC * 512].rearrange(
                    "p (k f) -> p k f", f=512)

            def gen_xc(ti, t0, w, after=0):
                b.wait("dve", "dve", max(after, b.count["dve"]))
                return b.tt(
                    "dve", xc3(ti % 2)[:, :, 0:w], x3[:, :, t0:t0 + w],
                    mb_sb[:, t0:t0 + w].unsqueeze(1).broadcast_to((128, KC, w)),
                    ALU.subtract)

            def load_masks(ti, t0, w):
                mp = ti % 2
                msem = "mk0" if mp == 0 else "mk1"
                b.wait("sp", "dsp", oh_w)
                for e in range(8):
                    mkload[mp] = b.dma(
                        "sp", msem,
                        mkbuf[:, mp * 4096 + e * 512: mp * 4096 + e * 512 + w],
                        oht_d[e:e + 1, t0:t0 + w].broadcast_to((128, w)))
                return mkload[mp]

            def emit_phase2_d(d, t0, w, p1cnt, p1acnt):
                eng = "dve" if d < 4 else "pool"
                sl_ = 0 if d < 4 else 1
                us = uscr[:, sl_ * 512: sl_ * 512 + w]
                b.wait(eng, "dve", p1cnt)
                b.wait(eng, "act", p1acnt)
                b.wait(eng, eng, b.count[eng])
                cc = b.tt(eng, us, x_sb[:, d * T + t0: d * T + t0 + w],
                          mb_sb[:, t0:t0 + w], ALU.subtract)
                b.wait(eng, eng, cc)
                cc = b.tt(eng, us, us, ev8[:, d * 512: d * 512 + w], ALU.add)
                b.wait(eng, eng, cc)
                cc = b.tt(eng, us, us, hrb_sb[:, t0:t0 + w], ALU.mult)
                b.wait(eng, eng, cc)
                cc = b.tt(eng, x_sb[:, d * T + t0: d * T + t0 + w],
                          x_sb[:, d * T + t0: d * T + t0 + w],
                          us, ALU.add)
                return cc, eng

            xc_dve = gen_xc(0, tcs[0][0], tcs[0][1])
            mk_w = load_masks(0, tcs[0][0], tcs[0][1])
            xc_next = xc_dve
            mk_next = mk_w
            lm_last = [0, 0]
            pending = []
            drain_w = {}
            ph2 = {"dve": 0, "pool": 0}
            for ti, (t0, w) in enumerate(tcs):
                xc_dve, mk_w = xc_next, mk_next
                mp = ti % 2
                msem = "mk0" if mp == 0 else "mk1"
                b.wait("pe", "dve", max(xc_dve, ph2["dve"]))
                b.wait("pe", "pool", ph2["pool"])
                for e in range(8):
                    if pending:
                        cc, eng = pending.pop(0)()
                        ph2[eng] = cc
                    if e == 5 and ti + 1 < len(tcs):
                        b.wait("sp", "dve", lm_last[(ti + 1) % 2])
                        mk_next = load_masks(ti + 1, tcs[ti + 1][0],
                                             tcs[ti + 1][1])
                    if e == 6 and ti + 1 < len(tcs):
                        xc_next = gen_xc(ti + 1, tcs[ti + 1][0], tcs[ti + 1][1])
                    for h in range(4):
                        buf = sidx % 5
                        if wfree[buf]:
                            b.wait("pool", "pe", wfree[buf])
                        if wtee[buf]:
                            b.wait("pool", tsem[buf], wtee[buf])
                        if ffn_idx == 0 and ti == 0:
                            dst = webuf[:, buf * 2048:
                                        (buf + 1) * 2048].rearrange(
                                "p (k f) -> p k f", f=1024)
                            src = wexp_d[e, h * 256:(h + 1) * 256, :].rearrange(
                                "(k p) f -> p k f", p=128)
                        else:
                            piece = e * 4 + h
                            if ffn_idx == 0:
                                b.wait("pool", piece_tee_sem[piece],
                                       piece_tee[piece])
                            else:
                                for ts_ in tsem:
                                    b.wait("pool", ts_, b.count[ts_])
                            dst = webuf[:, buf * 2048:(buf + 1) * 2048]
                            src = webf_d[e, :, h * 2048:(h + 1) * 2048]
                        wload[buf] += 1
                        b.dma("pool", wsem[buf], dst, src)
                        if ffn_idx == 0 and ti == 0:
                            b.wait("sp", wsem[buf], 16 * wload[buf])
                            wtee[buf] = b.dma(
                                "sp", tsem[buf],
                                webf_d[e, :, h * 2048:(h + 1) * 2048],
                                webuf[:, buf * 2048:(buf + 1) * 2048])
                            piece_tee[e * 4 + h] = wtee[buf]
                            piece_tee_sem[e * 4 + h] = tsem[buf]
                        b.wait("pe", wsem[buf], 16 * wload[buf])
                        sl2 = (e * 4 + h) % 4
                        if lmfree[sl2]:
                            b.wait("dve", "pe", lmfree[sl2])
                        b.wait("dve", "dve", xc_dve)
                        b.wait("dve", msem, mk_w)
                        lm3 = lmbuf[:, sl2 * 1024:(sl2 + 1) * 1024].rearrange(
                            "p (k f) -> p k f", f=512)
                        lmr = b.tt(
                            "dve", lm3[:, :, 0:w],
                            xc3(mp)[:, 2 * h:2 * h + 2, 0:w],
                            mkbuf[:, mp * 4096 + e * 512:
                                  mp * 4096 + e * 512 + w]
                            .unsqueeze(1).broadcast_to((128, 2, w)),
                            ALU.mult)
                        b.wait("pe", "dve", lmr)
                        pec = 0
                        for kk in range(2):
                            k = h * 2 + kk
                            first = (e == 0 and k == 0)
                            last = (e == 7 and k == 7)
                            for d in range(KC):
                                if first and d in drain_w:
                                    eng2, c2 = drain_w.pop(d)
                                    b.wait("pe", eng2, c2)
                                pec = b.mm(
                                    pd[:, d * 512: d * 512 + w],
                                    webuf[:, buf * 2048 + kk * 1024 + d * 128:
                                          buf * 2048 + kk * 1024 + (d + 1) * 128],
                                    lmbuf[:, sl2 * 1024 + kk * 512:
                                          sl2 * 1024 + kk * 512 + w],
                                    first, last)
                        lmfree[sl2] = pec
                        wfree[buf] = pec
                        sidx += 1
                lm_last[mp] = b.count["dve"]
                b.wait("dve", "pe", b.count["pe"])
                b.wait("act", "pe", b.count["pe"])
                p1d1 = b.cp("dve", ev3[:, 0:2, 0:w], pd3[:, 0:2, 0:w])
                p1 = b.cp("dve", ev3[:, 2:4, 0:w], pd3[:, 2:4, 0:w])
                p1a1 = b.cp("act", ev3[:, 4:6, 0:w], pd3[:, 4:6, 0:w])
                p1a = b.cp("act", ev3[:, 6:8, 0:w], pd3[:, 6:8, 0:w])
                drain_w.clear()
                drain_w.update({0: ("dve", p1d1), 2: ("dve", p1),
                                4: ("act", p1a1), 6: ("act", p1a)})
                pending = [
                    (lambda d=d, t0=t0, w=w, p1=p1, p1a=p1a:
                     emit_phase2_d(d, t0, w, p1, p1a))
                    for d in range(KC)]
                if ti + 1 == len(tcs):
                    for fn in pending:
                        cc, eng = fn()
                        ph2[eng] = cc
                    pending = []


    def emit_attn():
        with ExitStack() as ats:
            sb = lambda n, sh, dt: ats.enter_context(
                nc.sbuf_tensor(n + "_at", sh, dt))
            crep_sb = sb("crep", [128, R], BF16)
            srep_sb = sb("srep", [128, R], BF16)

            wbufA = sb("wbufA", [128, 8192], BF16)
            wbufB = sb("wbufB", [128, 8192], BF16)
            wb = {"A": [wbufA, "w0", 0, 0, 0, ""],
                  "B": [wbufB, "w1", 0, 0, 0, ""]}
            wb["A"][2] = b.count["w0"] // 16
            wb["B"][2] = b.count["w1"] // 16
            att_teed = [0, 0, 0, 0]

            def wload(which, m, src2d):
                ent = wb[which]
                if ent[3]:
                    b.wait("pool", "pe", ent[3])
                if ent[4]:
                    b.wait("pool", ent[5], ent[4])
                ent[2] += 1
                if att_teed[m]:
                    b.wait("pool", f"tee{m}", att_teed[m])
                    b.dma("pool", ent[1], ent[0][:], wqbf_d[m, :, :])
                else:
                    dst = ent[0][:].rearrange("p (k f) -> p k f", f=1024)
                    src = src2d.rearrange("(k p) f -> p k f", p=128)
                    b.dma("pool", ent[1], dst, src)
                    b.wait("sp", ent[1], 16 * ent[2])
                    att_teed[m] = b.dma("sp", f"tee{m}", wqbf_d[m, :, :],
                                        ent[0][:])
                    ent[4] = att_teed[m]
                    ent[5] = f"tee{m}"
                return 16 * ent[2]

            def wwait(which, v):
                b.wait("pe", wb[which][1], v)

            def wdone(which):
                wb[which][3] = b.count["pe"]

            nvt = R // 128
            k_sb = sb("k_sb", [128, KC * R], BF16)
            v_sb = sb("v_sb", [128, nvt * 1024], BF16)
            q_sb = sb("q_sb", [128, KC * cfg.qown], BF16)
            xc2 = sb("xc2", [128, KC * R], BF16)
            outq = sb("outq", [128, KC * 128], F32)
            qtw = cfg.qpq * 128
            attnq4 = sb("attnq4", [128, KC * qtw], BF16)
            qraw = sb("qraw", [128, 2048], BF16)
            rt1 = sb("rt1", [128, 1024], BF16)
            rt2 = qraw
            meanr = sb("ameanr", [1, R], F32)
            rstdr = sb("arstdr", [1, R], F32)
            rcolq = sb("rcolq", [128, nvt], F32)
            p32 = sb("p32", [128, 2 * 768], F32)
            pn_bf = sb("pn_bf", [128, 2 * 768], BF16)
            pt_sb = sb("pt_sb", [128, 2 * 768], BF16)
            nm2 = sb("nm2", [128, 4], F32)
            nm2s = sb("nm2s", [128, 4], F32)
            lr2 = sb("lr2", [128, 4], F32)
            rc2 = sb("rc2", [128, 4], F32)
            negc = sb("negc", [128, 1], F32)
            b.memset("dve", negc[:], -25.0)

            def produce_qk(Q, R0, dst, dst_w, which, subs, off):
                with ExitStack() as pscope:
                    pq = [pscope.enter_context(
                        nc.psum_tensor(f"pq{pp}{Q}{dst_w}", [128, 512], F32))
                        for pp in range(3)]
                    pw = [pscope.enter_context(
                        nc.psum_tensor(f"pw{pp}{Q}{dst_w}", [128, 512], F32))
                        for pp in range(3)]
                    wtens = wb[which][0]
                    units = [(d, s0, w) for d in range(KC) for (s0, w) in subs]
                    nu = len(units)
                    Mc = [0] * nu
                    Vc = [0] * nu
                    Wc = [0] * nu
                    Rc = [0] * nu
                    b.wait("dve", "dve", b.count["dve"])
                    b.wait("pe", "dve", b.count["dve"])
                    for i in range(nu + 5):
                        u = i - 4
                        if 0 <= u < nu:
                            d, s0, w = units[u]
                            p2 = u % 2
                            p3 = u % 3
                            p4 = u % 4
                            b.wait("dve", "pe", Wc[u])
                            b.wait("dve", "act", Vc[u])
                            c = b.tt("dve", rt1[:, p2 * 512: p2 * 512 + w],
                                     qraw[:, p4 * 512: p4 * 512 + w],
                                     crep_sb[:, off + s0: off + s0 + w],
                                     ALU.mult)
                            b.wait("dve", "dve", c)
                            c = b.tt("dve", qraw[:, p4 * 512: p4 * 512 + w],
                                     pw[p3][:, 0:w],
                                     srep_sb[:, off + s0: off + s0 + w],
                                     ALU.mult)
                            b.wait("dve", "dve", c)
                            Rc[u] = b.tt("dve",
                                         dst[:, d * dst_w + s0:
                                             d * dst_w + s0 + w],
                                         rt1[:, p2 * 512: p2 * 512 + w],
                                         qraw[:, p4 * 512: p4 * 512 + w],
                                         ALU.add)
                        u = i - 1
                        if 0 <= u < nu:
                            d, s0, w = units[u]
                            p3 = u % 3
                            p4 = u % 4
                            b.wait("act", "pe", Mc[u])
                            if u >= 4:
                                b.wait("act", "pe", Wc[u - 4])
                                b.wait("act", "dve", Rc[u - 4])
                            Vc[u] = b.cp("act", qraw[:, p4 * 512: p4 * 512 + w],
                                         pq[p3][:, 0:w])
                        u = i
                        if 0 <= u < nu:
                            d, s0, w = units[u]
                            p3 = u % 3
                            if u >= 3:
                                b.wait("pe", "act", Vc[u - 3])
                            for k in range(KC):
                                Mc[u] = b.mm(
                                    pq[p3][:, 0:w],
                                    wtens[:, k * 1024 + d * 128:
                                          k * 1024 + (d + 1) * 128],
                                    xc2[:, k * R + off + s0:
                                        k * R + off + s0 + w],
                                    k == 0, k == KC - 1)
                        u = i - 3
                        if 0 <= u < nu:
                            d, s0, w = units[u]
                            p3 = u % 3
                            p4 = u % 4
                            b.wait("pe", "act", Vc[u])
                            if u >= 3:
                                b.wait("pe", "dve", Rc[u - 3])
                            Wc[u] = b.mm(pw[p3][:, 0:w], pswap_sb[:],
                                         qraw[:, p4 * 512: p4 * 512 + w],
                                         True, True)
                    b.wait("pe", "dve", b.count["dve"])
                    b.wait("pe", "act", b.count["act"])

            prev_outq = 0
            wout_pe = 0
            b.wait("pool", "dve", b.count["dve"])
            b.dma("pool", "dpl", crep_sb[:], crep_d[:, 0:R])
            b.dma("pool", "dpl", srep_sb[:], srep_d[:, 0:R])
            cs_pl = b.count["dpl"]
            lv = wload("A", 2, wqkv_d[:, 2048:3072])
            lq = wload("B", 0, wqkv_d[:, 0:1024])
            for Q in range(cfg.nq):
                R0 = Q * cfg.qown
                b.wait("dve", "dpl", cs_pl)
                emit_stats(f"a{Q}", R0, R, meanr[:], rstdr[:], None)
                with ExitStack() as pscope:
                    pb = pscope.enter_context(
                        nc.psum_tensor(f"pba{Q}", [128, 512], F32))
                    emit_cols(rstdr[:], nvt, rcolq, pb)
                    for (t0, w) in cfg.rsubs():
                        b.wait("pe", "dve", b.count["dve"])
                        b.wait("pe", "act", b.count["act"])
                        mmc = b.mm(pb[:, 0:w], ones1[:],
                                   rstdr[0:1, t0:t0 + w], True, True)
                        b.wait("dve", "pe", mmc)
                        c = b.tt("dve", crep_sb[:, t0:t0 + w],
                                 crep_sb[:, t0:t0 + w], pb[:, 0:w], ALU.mult)
                        b.wait("dve", "dve", c)
                        c = b.tt("dve", srep_sb[:, t0:t0 + w],
                                 srep_sb[:, t0:t0 + w], pb[:, 0:w], ALU.mult)
                        b.wait("pe", "dve", c)
                    for (t0, w) in cfg.rsubs():
                        b.wait("pe", "dve", b.count["dve"])
                        mmc = b.mm(pb[:, 0:w], ones1[:], meanr[0:1, t0:t0 + w],
                                   True, True)
                        b.wait("dve", "pe", mmc)
                        ev = 0
                        for k in range(KC):
                            ev = b.tt("dve",
                                      xc2[:, k * R + t0: k * R + t0 + w],
                                      x_sb[:, k * T + R0 + t0:
                                           k * T + R0 + t0 + w],
                                      pb[:, 0:w], ALU.subtract)
                        b.wait("pe", "dve", ev)
                if Q > 0:
                    b.wait("dve", "dve", b.count["dve"])
                    p0 = Q * cfg.qown
                    for d in range(KC):
                        b.tt("dve", x_sb[:, d * T + p0: d * T + p0 + 128],
                             x_sb[:, d * T + p0: d * T + p0 + 128],
                             outq[:, d * 128:(d + 1) * 128], ALU.add)

                with ExitStack() as pscope:
                    pv = pscope.enter_context(
                        nc.psum_tensor(f"pv{Q}", [128, 1024], F32))
                    ev_sl = [0, 0]
                    b.wait("pe", "dve", b.count["dve"])
                    b.wait("dve", "dve", b.count["dve"])
                    wwait("A", lv)
                    for tt in range(nvt):
                        for vc in range(2):
                            sl = (tt * 2 + vc) % 2
                            if ev_sl[sl]:
                                b.wait("pe", "dve", ev_sl[sl])
                            for k in range(KC):
                                mmc = b.mm(
                                    pv[:, sl * 512: sl * 512 + 512],
                                    xc2[:, k * R + tt * 128:
                                        k * R + (tt + 1) * 128],
                                    wbufA[:, k * 1024 + vc * 512:
                                          k * 1024 + (vc + 1) * 512],
                                    k == 0, k == KC - 1)
                            b.wait("dve", "pe", mmc)
                            ev_sl[sl] = b.ts(
                                "dve",
                                v_sb[:, tt * 1024 + vc * 512:
                                     tt * 1024 + (vc + 1) * 512],
                                pv[:, sl * 512: sl * 512 + 512],
                                rcolq[:, tt: tt + 1], ALU.mult)
                    b.wait("pe", "dve", max(ev_sl))
                    wdone("A")
                lk = wload("A", 1, wqkv_d[:, 1024:2048])

                wwait("B", lq)
                qsubs = [(s, min(512, cfg.qown - s))
                         for s in range(0, cfg.qown, 512)]
                produce_qk(Q, R0, q_sb, cfg.qown, "B", qsubs, 128)
                wdone("B")
                lo = wload("B", 3, wout_d[:, :])
                wwait("A", lk)
                produce_qk(Q, R0, k_sb, R, "A", cfg.rsubs(), 0)
                wdone("A")
                if Q + 1 < cfg.nq:
                    b.wait("pool", "dve", b.count["dve"])
                    R0n = (Q + 1) * cfg.qown
                    b.dma("pool", "dpl", crep_sb[:], crep_d[:, R0n:R0n + R])
                    b.dma("pool", "dpl", srep_sb[:], srep_d[:, R0n:R0n + R])
                    cs_pl = b.count["dpl"]
                    lv = wload("A", 2, wqkv_d[:, 2048:3072])

                wwait("B", lo)
                with ExitStack() as pscope:
                    ps_s = pscope.enter_context(
                        nc.psum_tensor(f"pss{Q}", [128, 2048], F32))
                    ps_pt = [pscope.enter_context(
                        nc.psum_tensor(f"pspt{pp}{Q}", [128, 768], BF16))
                        for pp in range(2)]
                    ps_o = [pscope.enter_context(
                        nc.psum_tensor(f"pso{pp}{Q}", [128, 128], F32))
                        for pp in range(2)]
                    n = cfg.qpq * 8
                    slots = [(qi, hp) for qi in range(cfg.qpq)
                             for hp in range(8)]
                    Acnt = [0] * n
                    Bcnt = [0] * n
                    Ccnt = [0] * n
                    Dcnt = [0] * n
                    Ecnt = [0] * n
                    Fcnt = [0] * n
                    Gcnt = [0] * n
                    Hcnt = [0] * n
                    Xd = [0] * n

                    def bsel_of(j):
                        gqt = Q * cfg.qpq + slots[j][0]
                        return 1 if gqt == 0 else (
                            2 if gqt == cfg.last_qt else 0)

                    b.wait("pe", "dve", b.count["dve"])
                    for i in range(n + 8):
                        j = i - 2
                        if 0 <= j < n:
                            pr = j % 2
                            b.wait("act", "pe", Acnt[j])
                            if j >= 2:
                                b.wait("act", "dve", Dcnt[j - 2])
                            for jh in range(2):
                                Ccnt[j] = b.actv(
                                    p32[:, pr * 768 + jh * 384:
                                        pr * 768 + (jh + 1) * 384],
                                    ps_s[:, (2 * pr + jh) * 512:
                                         (2 * pr + jh) * 512 + 384],
                                    AF.Exp, bias=negc[:, 0:1],
                                    scale=0.125,
                                    accum=lr2[:, 2 * pr + jh:2 * pr + jh + 1])
                        if i < n:
                            qi, hp = slots[i]
                            bsel = bsel_of(i)
                            if i >= 2:
                                b.wait("pe", "act", Ccnt[i - 2])
                            for jh in range(2):
                                hb = jh * 64
                                sb2 = ((i % 2) * 2 + jh) * 512
                                b.mm(ps_s[:, sb2: sb2 + 384], idb[:],
                                     biasq_sb[:, bsel * 384:(bsel + 1) * 384],
                                     True, False)
                                Acnt[i] = b.mm(
                                    ps_s[:, sb2: sb2 + 384],
                                    q_sb[hb:hb + 64, hp * cfg.qown + qi * 128:
                                         hp * cfg.qown + (qi + 1) * 128],
                                    k_sb[hb:hb + 64, hp * R + qi * 128:
                                         hp * R + qi * 128 + 384],
                                    False, True)
                        j = i - 3
                        if 0 <= j < n:
                            pr = j % 2
                            b.wait("dve", "act", Ccnt[j])
                            c = b.recip(rc2[:, 2 * pr:2 * pr + 2],
                                        lr2[:, 2 * pr:2 * pr + 2])
                            b.wait("dve", "dve", c)
                            if j >= 2:
                                b.wait("dve", "pe", Ecnt[j - 2])
                            for jh in range(2):
                                Dcnt[j] = b.ts(
                                    "dve",
                                    pn_bf[:, pr * 768 + jh * 384:
                                          pr * 768 + (jh + 1) * 384],
                                    p32[:, pr * 768 + jh * 384:
                                        pr * 768 + (jh + 1) * 384],
                                    rc2[:, 2 * pr + jh:2 * pr + jh + 1],
                                    ALU.mult)
                        j = i - 4
                        if 0 <= j < n:
                            pr = j % 2
                            b.wait("pe", "dve", Dcnt[j])
                            if j >= 2:
                                b.wait("pe", "act", Fcnt[j - 2])
                            for kb in range(6):
                                Ecnt[j] = b.tr(
                                    ps_pt[pr][:, kb * 128:(kb + 1) * 128],
                                    pn_bf[:, pr * 768 + kb * 128:
                                          pr * 768 + (kb + 1) * 128],
                                    idb[:])
                        j = i - 5
                        if 0 <= j < n:
                            pr = j % 2
                            b.wait("act", "pe", Ecnt[j])
                            if j >= 2:
                                b.wait("act", "pe", Gcnt[j - 2])
                            Fcnt[j] = b.cp("act",
                                           pt_sb[:, pr * 768:(pr + 1) * 768],
                                           ps_pt[pr][:])
                        j = i - 6
                        if 0 <= j < n:
                            pr = j % 2
                            qi1, hp1 = slots[j]
                            b.wait("pe", "act", Fcnt[j])
                            if j >= 2:
                                b.wait("pe", "act", Hcnt[j - 2])
                            for jh in range(2):
                                h1 = hp1 * 2 + jh
                                hb1 = jh * 64
                                for kb in range(3):
                                    Gcnt[j] = b.mm(
                                        ps_o[pr][hb1:hb1 + 64, :],
                                        v_sb[:, (qi1 + kb) * 1024 + h1 * 64:
                                             (qi1 + kb) * 1024 +
                                             (h1 + 1) * 64],
                                        pt_sb[:, pr * 768 + jh * 384 +
                                              kb * 128:
                                              pr * 768 + jh * 384 +
                                              (kb + 1) * 128],
                                        kb == 0, kb == 2,
                                        tile_position=(0, hb1))
                        j = i - 7
                        if 0 <= j < n:
                            pr = j % 2
                            qi1, hp1 = slots[j]
                            b.wait("act", "pe", Gcnt[j])
                            if wout_pe:
                                b.wait("act", "pe", wout_pe)
                            Hcnt[j] = b.cp(
                                "act",
                                attnq4[:, hp1 * qtw + qi1 * 128:
                                       hp1 * qtw + (qi1 + 1) * 128],
                                ps_o[pr][:])
                    b.wait("pe", "act", b.count["act"])
                    wev = {0: 0, 1: 0}
                    for dout in range(KC):
                        wp = dout % 2
                        if wev[wp]:
                            b.wait("pe", "dve", wev[wp])
                        for k in range(KC):
                            mmc = b.mm(ps_s[:, wp * 1024: wp * 1024 + qtw],
                                       wbufB[:, k * 1024 + dout * 128:
                                             k * 1024 + (dout + 1) * 128],
                                       attnq4[:, k * qtw:(k + 1) * qtw],
                                       k == 0, k == KC - 1)
                        b.wait("dve", "pe", mmc)
                        b.wait("dve", "dve", b.count["dve"])
                        abs0 = 128 + Q * cfg.qown
                        nlast = qtw - 128
                        if nlast > 0:
                            b.stt("dve",
                                  x_sb[:, dout * T + abs0:
                                       dout * T + abs0 + nlast],
                                  ps_s[:, wp * 1024: wp * 1024 + nlast],
                                  bout_sb[:, dout:dout + 1],
                                  x_sb[:, dout * T + abs0:
                                       dout * T + abs0 + nlast],
                                  ALU.add, ALU.add)
                        wev[wp] = b.ts(
                            "dve",
                            outq[:, dout * 128:(dout + 1) * 128],
                            ps_s[:, wp * 1024 + nlast:
                                 wp * 1024 + nlast + 128],
                            bout_sb[:, dout:dout + 1], ALU.add)
                    wout_pe = b.count["pe"]
                    wdone("B")
                if Q + 1 < cfg.nq:
                    lq = wload("B", 0, wqkv_d[:, 0:1024])
                prev_outq = b.count["dve"]
            b.wait("dve", "dve", b.count["dve"])
            p0 = cfg.nq * cfg.qown
            for d in range(KC):
                b.tt("dve", x_sb[:, d * T + p0: d * T + p0 + 128],
                     x_sb[:, d * T + p0: d * T + p0 + 128],
                     outq[:, d * 128:(d + 1) * 128], ALU.add)

    def barrier():
        for e in ENGINES:
            for s in ("pe", "dve", "act", "pool", "dsp", "dpl",
                      "w0", "w1", "w2", "w3", "w4",
                      "tee0", "tee1", "tee2", "tee3", "tee4"):
                b.wait(e, s)

    if "ffn0" in phases:
        emit_ffn(0)
        barrier()
    if "attn" in phases:
        emit_attn()
        barrier()
    if "ffn1" in phases:
        emit_ffn(1)
        barrier()

    b.wait("sp", "dve", b.count["dve"])
    b.wait("sp", "pool", b.count["pool"])
    for d in range(KC):
        b.dma("sp", "dso", out_d[d * 128:(d + 1) * 128, :],
              x_sb[:, d * T + 128: d * T + 128 + cfg.own])
    b.wait("sp", "dso", b.count["dso"])

    with nc.Block() as block:
        @block.tensor
        def _(t):
            for fn in b.ops["pe"]:
                fn(t)

        @block.vector
        def _(v):
            for fn in b.ops["dve"]:
                fn(v)

        @block.scalar
        def _(s):
            for fn in b.ops["act"]:
                fn(s)

        @block.gpsimd
        def _(g):
            for fn in b.ops["pool"]:
                fn(g)

        @block.sync
        def _(s):
            for fn in b.ops["sp"]:
                fn(s)

    stack.close()
    return nc


def make_consts():
    ident = np.eye(128, dtype=np.float32)
    pswap = np.zeros((128, 128), dtype=np.float32)
    for j in range(128):
        i = (j + 32) % 64 + 64 * (j // 64)
        pswap[i, j] = 1.0
    return ident, pswap


def make_band_bias(left_edge, right_edge):
    i = np.arange(128)[:, None]
    j = np.arange(384)[None, :]
    band = np.where((j >= i) & (j <= i + 256), 0.0, NEG).astype(np.float32)
    first = band.copy()
    if left_edge:
        first[:, 0:128] = NEG
    last = band.copy()
    if right_edge:
        last[:, 256:384] = NEG
    out = np.ascontiguousarray(np.concatenate([band, first, last], axis=1))
    for v in range(3):
        assert np.all(out[:, v * 384 + 128: v * 384 + 256] == 0.0)
    return out


def make_rope(positions):
    inv = 1.0 / (10000.0 ** (np.arange(0, ROT * 2, 2, dtype=np.float32)
                             / (ROT * 2)))
    fr = positions[None, :].astype(np.float32) * inv[:, None]
    c = np.cos(fr).astype(np.float32)
    s = np.sin(fr).astype(np.float32)
    crep = np.ascontiguousarray(np.concatenate([c, c, c, c], axis=0))
    srep = np.ascontiguousarray(np.concatenate([-s, s, -s, s], axis=0))
    return crep, srep


def core_inputs(cfg: CFG, x_seq, seq_start, seq_len):
    own, halo, T = cfg.own, cfg.halo, cfg.T
    xe = np.zeros((T, D), dtype=np.float32)
    xe[halo:halo + own] = x_seq[seq_start:seq_start + own]
    left_edge = seq_start < halo
    if not left_edge:
        xe[0:halo] = x_seq[seq_start - halo:seq_start]
    right_edge = seq_start + own + halo > seq_len
    if not right_edge:
        xe[halo + own:] = x_seq[seq_start + own:seq_start + own + halo]
    xT = np.ascontiguousarray(xe.T)
    pos = np.arange(seq_start - halo, seq_start + own + halo)
    crep, srep = make_rope(pos)
    biasq = make_band_bias(left_edge, right_edge)
    return xT, crep, srep, biasq


def kernel(L0_attn_x, L0_attn_Wqkv, L0_attn_Wout, L0_attn_bout,
           L0_ffn_gate_w, L0_ffn_expert_w):
    cfg = CFG(own=2048, qown=512)
    x = np.asarray(L0_attn_x, dtype=np.float32)
    wqkv = np.ascontiguousarray(np.asarray(L0_attn_Wqkv, dtype=np.float32))
    wout = np.ascontiguousarray(np.asarray(L0_attn_Wout, dtype=np.float32))
    bout = np.asarray(L0_attn_bout, dtype=np.float32)
    gw = np.asarray(L0_ffn_gate_w, dtype=np.float32)
    wexp = np.asarray(L0_ffn_expert_w, dtype=np.float32)

    boutT = np.ascontiguousarray(bout.reshape(8, 128).T)
    gwT = np.ascontiguousarray(gw.T)
    wexpT = np.ascontiguousarray(wexp.transpose(0, 2, 1))
    ident, pswap = make_consts()

    B_, S, _ = x.shape
    n_cores = 8
    per_batch = n_cores // B_
    in_maps = []
    for c in range(n_cores):
        bidx, qidx = c // per_batch, c % per_batch
        xT, crep, srep, biasq = core_inputs(cfg, x[bidx], qidx * cfg.own, S)
        in_maps.append({
            "xT": xT, "wqkv": wqkv, "wout": wout, "boutT": boutT,
            "gwT": gwT, "wexpT": wexpT, "ident": ident,
            "pswap": pswap, "crep": crep, "srep": srep, "biasq": biasq,
        })

    nc = build_program(cfg)
    res = run_bass_kernel_spmd(nc, in_maps, core_ids=list(range(n_cores)))
    out = np.empty_like(x)
    for c in range(n_cores):
        bidx, qidx = c // per_batch, c % per_batch
        out[bidx, qidx * cfg.own:(qidx + 1) * cfg.own, :] = \
            res.results[c]["outT"].T
    return out
